# version 19
# baseline (speedup 1.0000x reference)
import numpy as np
import concourse.bass as bass
import concourse.mybir as mybir
from concourse.bass_utils import run_bass_kernel_spmd

F32 = mybir.dt.float32
BF16 = mybir.dt.bfloat16
AF = mybir.ActivationFunctionType
ALU = mybir.AluOpType

NCORES = 8
DBG = False
P = 128
NS = 32
T = 2048
TS = 512
NT = 4
DM = 4096
CAP = 6144.0
ALPHA = 1.0 / 16.0
N_SMALL = 13


def _build():
    nc = bass.Bass("TRN2", target_bir_lowering=False, detect_race_conditions=False)

    d_xT = nc.dram_tensor("xT", [DM, T], F32, kind="ExternalInput")
    d_wT = nc.dram_tensor("wT", [NS, P, NS, P], F32, kind="ExternalInput")
    d_lA = nc.dram_tensor("lA", [NS, P, P], F32, kind="ExternalInput")
    d_lB = nc.dram_tensor("lB", [NS, P, P], F32, kind="ExternalInput")
    d_g1 = nc.dram_tensor("g1", [NS, P, 32], F32, kind="ExternalInput")
    d_g2T = nc.dram_tensor("g2T", [32, 8], F32, kind="ExternalInput")
    d_gb1 = nc.dram_tensor("gb1", [32, 1], F32, kind="ExternalInput")
    d_lng = nc.dram_tensor("lng", [32, 1], F32, kind="ExternalInput")
    d_lnb = nc.dram_tensor("lnb", [32, 1], F32, kind="ExternalInput")
    d_gb2 = nc.dram_tensor("gb2", [8, 1], F32, kind="ExternalInput")
    d_ssm = nc.dram_tensor("ssm", [64, 9], F32, kind="ExternalInput")
    d_vcr = nc.dram_tensor("vcr", [64, 64], F32, kind="ExternalInput")
    d_vci = nc.dram_tensor("vci", [64, 64], F32, kind="ExternalInput")
    d_dsc = nc.dram_tensor("dsc", [1, 2], F32, kind="ExternalInput")
    d_out = nc.dram_tensor("outT", [DM, T], F32, kind="ExternalOutput")
    if DBG:
        d_dbg = nc.dram_tensor("dbg", [P, 8192], F32, kind="ExternalOutput")
    d_ccin = nc.dram_tensor("ccin", [1, 16], F32)
    d_ccout = nc.dram_tensor("ccout", [1, 16], F32)

    import contextlib
    es = contextlib.ExitStack()
    sb = lambda name, shape, dt: es.enter_context(nc.sbuf_tensor(name, shape, dt))

    actbuf = sb("actbuf", [P, NS * T], BF16)
    whbuf = sb("whbuf", [P, 2 * 16 * P], BF16)
    lBb = sb("lBb", [P, NS * P], BF16)
    lAx = sb("lAx", [P, NS * P], BF16)
    g1b = sb("g1b", [P, NS * 32], BF16)
    scr_a = sb("scr_a", [P, T], F32)
    tmp_bf = sb("tmp_bf", [P, T], BF16)
    tmp_sc = sb("tmp_sc", [P, T], BF16)
    gh = sb("gh", [32, T], BF16)
    lAx2 = sb("lAx2", [P, T], BF16)
    rowscr = sb("rowscr", [1, 1024], BF16)
    i128f = sb("i128f", [P, P], F32)
    i128b = sb("i128b", [P, P], BF16)
    i8f = sb("i8f", [8, 8], F32)
    i8b = sb("i8b", [8, 8], BF16)
    iot_f = sb("iot_f", [P, P], F32)
    iot_p = sb("iot_p", [P, 1], F32)
    iot8 = sb("iot8", [P, 8], F32)
    c8mi = sb("c8mi", [P, 8], F32)
    esel = sb("esel", [8, P], BF16)
    ones_bf = sb("ones_bf", [P, 1], BF16)
    ones_f = sb("ones_f", [P, 1], F32)
    onesr_bf = sb("onesr_bf", [1, P], BF16)
    onesr_f = sb("onesr_f", [1, P], F32)
    Tkb = sb("Tkb", [P, P], BF16)
    Pbig = sb("Pbig", [P, P], BF16)
    Qbig = sb("Qbig", [P, P], BF16)
    Mhi = sb("Mhi", [P, P], BF16)
    Mlo = sb("Mlo", [P, P], BF16)
    gt = sb("gt", [P, P], F32)
    oh1 = sb("oh1", [P, P], BF16)
    oh2 = sb("oh2", [P, P], BF16)
    wtok = sb("wtok", [P, P], BF16)
    d16 = sb("d16", [P, 16], F32)
    e16 = sb("e16", [P, 16], F32)
    tg1c = sb("tg1c", [P, 16], F32)
    tg2c = sb("tg2c", [P, 16], F32)
    alb = sb("alb", [P, 16], F32)
    cnt_sb = sb("cnt_sb", [8, 2], F32)
    ccs = sb("ccs", [1, 16], F32)
    cct = sb("cct", [1, 16], F32)
    alr = sb("alr", [1, 16], F32)
    Krow = sb("Krow", [1, P], F32)
    kcol = sb("kcol", [P, 1], F32)
    stp1 = sb("stp1", [1, 1], F32)
    dD1 = sb("dD1", [1, 1], F32)
    gb1s = sb("gb1s", [32, 1], F32)
    lngs = sb("lngs", [32, 1], F32)
    lnbs = sb("lnbs", [32, 1], F32)
    gb2s = sb("gb2s", [8, 1], F32)
    g2f = sb("g2f", [32, 8], F32)
    g2b = sb("g2b", [32, 8], BF16)

    es2 = contextlib.ExitStack()
    sb2 = lambda name, shape, dt: es2.enter_context(nc.sbuf_tensor(name, shape, dt))
    MR = [sb2(f"MR{k}", [64, 64], F32) for k in range(10)]
    MI = [sb2(f"MI{k}", [64, 64], F32) for k in range(10)]
    MTR = [sb2(f"MTR{k}", [64, 64], F32) for k in range(10)]
    MTI = [sb2(f"MTI{k}", [64, 64], F32) for k in range(10)]
    negscr = sb2("negscr", [64, 64], F32)
    negscr2 = sb2("negscr2", [64, 64], F32)
    vec = sb2("vecs", [64, 40], F32)
    Pr = sb2("Pr", [64, P], F32)
    Pi = sb2("Pi", [64, P], F32)
    Wr = sb2("Wr", [64, P], F32)
    Wi = sb2("Wi", [64, P], F32)
    Tk32 = sb2("Tk32", [P, P], F32)
    i64f = sb2("i64f", [64, 64], F32)
    arow = sb2("arow", [1, 4 * 64], F32)
    mrow = sb2("mrow", [1, 2 * 64], F32)
    sc11 = sb2("sc11", [1, 8], F32)
    es2.close()

    S_hi = sb("S_hi", [P, T], BF16)
    S_lo = sb("S_lo", [P, T], BF16)
    es3 = contextlib.ExitStack()
    map_mu = es3.enter_context(nc.sbuf_tensor("map_mu", [P, T], BF16))
    map_rs = es3.enter_context(nc.sbuf_tensor("map_rs", [P, T], BF16))
    es3.close()
    gatesf = sb("gatesf", [8, T], F32)

    def V(i):
        return vec[:, i:i + 1]

    (LAM_R, LAM_I, P_R, P_I, Q_R, Q_I, CT_R, CT_I, BP_, STP64, HSTP,
     DM_R, DM_I, DP_R, DP_I, DMI_R, DMI_I, U_R, U_I, A_R, A_I,
     B_R, B_I, BT_R, BT_I, G_R, G_I, M_R, M_I, T0, T1, T2, T3,
     V_R, V_I, W_R, W_I, R_R, R_I) = range(39)

    esp = contextlib.ExitStack()
    pb = lambda name: esp.enter_context(nc.psum_tensor(name, [P, TS], F32))
    ob = [pb(f"ob{t}") for t in range(NT)]
    sbk = [pb("sbk0"), pb("sbk1")]
    miscp = pb("miscp")
    sparep = pb("sparep")
    hbanks = [sbk[0], sbk[1], miscp, sparep]

    ess = contextlib.ExitStack()
    sem = lambda name: ess.enter_context(nc.semaphore(name))
    ch = sem("ch")
    d_small = sem("d_small")
    d_gsm = sem("d_gsm")
    d_xbf = sem("d_xbf")
    d_w = sem("d_w")
    d_xr = sem("d_xr")
    d_outS = sem("d_outS")
    d_cc = sem("d_cc")
    d_tk = sem("d_tk")
    c_cc = sem("c_cc")
    g_xn = sem("g_xn")
    t_st = sem("t_st")
    v_sq = sem("v_sq")
    v_xn = sem("v_xn")
    s_wfree = sem("s_wfree")
    s_main = sem("s_main")
    s_evac = sem("s_evac")
    s_scanm = sem("s_scanm")
    sc_shi = sem("sc_shi")
    s_split = sem("s_split")

    plan = []
    CHN = [0]
    CH_IDX = {}

    def chain(eng, name=None, serial=True):
        def deco(fn):
            plan.append([eng, "chain", fn, serial, name])
            CHN[0] += 1
        return deco

    def free(eng, name=None):
        def deco(fn):
            plan.append([eng, "free", fn, False, name])
        return deco

    def finalize_plan():
        CH_IDX.clear()
        ci = 0
        for e in plan:
            if e[1] == "chain":
                if e[4]:
                    CH_IDX[e[4]] = ci
                ci += 1

    def xn_ap(s, t0, n):
        return actbuf[:, s * T + t0: s * T + t0 + n]

    def cmul_vec(v_, or_, oi_, ar, ai, br, bi):
        v_.tensor_tensor(out=V(T0), in0=V(ar), in1=V(br), op=ALU.mult)
        v_.tensor_tensor(out=V(T1), in0=V(ai), in1=V(bi), op=ALU.mult)
        v_.tensor_tensor(out=V(or_), in0=V(T0), in1=V(T1), op=ALU.subtract)
        v_.tensor_tensor(out=V(T0), in0=V(ar), in1=V(bi), op=ALU.mult)
        v_.tensor_tensor(out=V(T1), in0=V(ai), in1=V(br), op=ALU.mult)
        v_.tensor_tensor(out=V(oi_), in0=V(T0), in1=V(T1), op=ALU.add)

    @free("sync")
    def _(sync):
        aps = [
            (vec[:, LAM_R:LAM_R + 9], d_ssm.ap()[:, :]),
            (MR[8][:, :], d_vcr.ap()[:, :]),
            (MI[8][:, :], d_vci.ap()[:, :]),
            (dD1[:, :], d_dsc.ap()[:, 0:1]),
            (stp1[:, :], d_dsc.ap()[:, 1:2]),
            (gb1s[:, :], d_gb1.ap()[:, :]),
            (lngs[:, :], d_lng.ap()[:, :]),
            (lnbs[:, :], d_lnb.ap()[:, :]),
            (gb2s[:, :], d_gb2.ap()[:, :]),
            (g2f[:, :], d_g2T.ap()[:, :]),
        ]
        assert len(aps) == 10
        for dst, src in aps:
            sync.dma_start(out=dst, in_=src, allow_slow_non_contiguous=True).then_inc(d_small, 16)

    NSMALL = 10

    @chain("gpsimd", "iotas")
    def _(g):
        g.iota(iot_f[:, :], pattern=[[1, P]], base=0, channel_multiplier=0,
               allow_small_or_imprecise_dtypes=True)
        g.iota(iot_p[:, :], pattern=[[0, 1]], base=0, channel_multiplier=1,
               allow_small_or_imprecise_dtypes=True)
        g.iota(iot8[:, :], pattern=[[1, 8]], base=0, channel_multiplier=0,
               allow_small_or_imprecise_dtypes=True)

    @chain("vector", "consts")
    def _(v):
        v.tensor_scalar(out=i128f[:, :], in0=iot_f[:, :], scalar1=iot_p[:, 0:1],
                        scalar2=None, op0=ALU.is_equal)
        v.tensor_copy(out=i128b[:, :], in_=i128f[:, :])
        v.tensor_scalar(out=i8f[:, :], in0=iot_f[0:8, 0:8], scalar1=iot_p[0:8, 0:1],
                        scalar2=None, op0=ALU.is_equal)
        v.tensor_copy(out=i8b[:, :], in_=i8f[:, :])
        v.tensor_scalar(out=i64f[:, :], in0=iot_f[0:64, 0:64], scalar1=iot_p[0:64, 0:1],
                        scalar2=None, op0=ALU.is_equal)
        v.tensor_scalar(out=c8mi[:, :], in0=iot8[:, :], scalar1=-1.0, scalar2=8.0,
                        op0=ALU.mult, op1=ALU.add)
        v.memset(ones_bf[:, :], 1.0)
        v.memset(ones_f[:, :], 1.0)
        v.memset(onesr_bf[:, :], 1.0)
        v.memset(onesr_f[:, :], 1.0)
        v.wait_ge(d_small, 16 * NSMALL)
        v.tensor_copy(out=g2b[:, :], in_=g2f[:, :])

    @chain("gpsimd", "esel_iota")
    def _(g):
        g.iota(Tk32[0:8, 0:P], pattern=[[1, 8], [0, 16]], base=0,
               channel_multiplier=0, allow_small_or_imprecise_dtypes=True)

    @chain("vector", "esel")
    def _(v):
        v.tensor_scalar(out=esel[:, :], in0=Tk32[0:8, 0:P], scalar1=iot_p[0:8, 0:1],
                        scalar2=ALPHA, op0=ALU.is_equal, op1=ALU.mult)

    @free("gpsimd")
    def _(g):
        xr = d_xT.ap().rearrange("(s p) t -> s p t", p=P)
        for s in range(NS):
            g.dma_start(out=actbuf[:, s * T:(s + 1) * T], in_=xr[s]).then_inc(d_xbf, 16)
        g.dma_start(out=lAx[:, :].rearrange("p (s e) -> p s e", s=NS), in_=d_lA.ap().rearrange("s p e -> p s e")).then_inc(d_gsm, 16)
        g.dma_start(out=lBb[:, :].rearrange("e (f o) -> e f o", f=NS), in_=d_lB.ap().rearrange("f e o -> e f o")).then_inc(d_gsm, 16)
        g.dma_start(out=g1b[:, :].rearrange("p (s j) -> p s j", s=NS), in_=d_g1.ap().rearrange("s p j -> p s j")).then_inc(d_gsm, 16)
        wr = d_wT.ap()
        for H in range(2):
            f, h = H // 2, H % 2
            g.dma_start(
                out=whbuf[:, (H % 2) * 16 * P:(H % 2 + 1) * 16 * P],
                in_=wr[f, :, 16 * h:16 * (h + 1), :].rearrange("p s o -> p (s o)"),
            ).then_inc(d_w, 16)

    @chain("scalar", "stp")
    def _(s):
        s.wait_ge(d_small, 16 * NSMALL)
        s.activation(out=stp1[:, :], in_=stp1[:, :], func=AF.Exp)

    @chain("tensor", "stp_bcast")
    def _(t):
        t.matmul(miscp[0:64, 0:1], onesr_f[0:1, 0:64], stp1[0:1, 0:1], start=True, stop=True)

    @chain("vector", "stp_vec")
    def _(v):
        v.tensor_copy(out=V(STP64), in_=miscp[0:64, 0:1])
        v.tensor_scalar(out=V(HSTP), in0=V(STP64), scalar1=0.5, scalar2=None, op0=ALU.mult)
        v.tensor_tensor(out=V(T0), in0=V(HSTP), in1=V(LAM_R), op=ALU.mult)
        v.tensor_tensor(out=V(T1), in0=V(HSTP), in1=V(LAM_I), op=ALU.mult)
        v.tensor_scalar(out=V(DM_R), in0=V(T0), scalar1=-1.0, scalar2=1.0, op0=ALU.mult, op1=ALU.add)
        v.tensor_scalar(out=V(DM_I), in0=V(T1), scalar1=-1.0, scalar2=None, op0=ALU.mult)
        v.tensor_scalar(out=V(DP_R), in0=V(T0), scalar1=1.0, scalar2=None, op0=ALU.add)
        v.tensor_copy(out=V(DP_I), in_=V(T1))
        v.tensor_tensor(out=V(T0), in0=V(DM_R), in1=V(DM_R), op=ALU.mult)
        v.tensor_tensor(out=V(T1), in0=V(DM_I), in1=V(DM_I), op=ALU.mult)
        v.tensor_tensor(out=V(T2), in0=V(T0), in1=V(T1), op=ALU.add)
        v.reciprocal(out=V(T3), in_=V(T2))
        v.tensor_tensor(out=V(DMI_R), in0=V(DM_R), in1=V(T3), op=ALU.mult)
        v.tensor_tensor(out=V(T0), in0=V(DM_I), in1=V(T3), op=ALU.mult)
        v.tensor_scalar(out=V(DMI_I), in0=V(T0), scalar1=-1.0, scalar2=None, op0=ALU.mult)
        v.tensor_tensor(out=V(U_R), in0=V(HSTP), in1=V(P_R), op=ALU.mult)
        v.tensor_tensor(out=V(U_I), in0=V(HSTP), in1=V(P_I), op=ALU.mult)
        cmul_vec(v, A_R, A_I, DMI_R, DMI_I, U_R, U_I)
        v.tensor_tensor(out=V(T0), in0=V(Q_R), in1=V(DMI_R), op=ALU.mult)
        v.tensor_tensor(out=V(T1), in0=V(Q_I), in1=V(DMI_I), op=ALU.mult)
        v.tensor_tensor(out=V(R_R), in0=V(T0), in1=V(T1), op=ALU.add)
        v.tensor_tensor(out=V(T0), in0=V(Q_R), in1=V(DMI_I), op=ALU.mult)
        v.tensor_tensor(out=V(T1), in0=V(Q_I), in1=V(DMI_R), op=ALU.mult)
        v.tensor_tensor(out=V(R_I), in0=V(T0), in1=V(T1), op=ALU.subtract)
        v.tensor_tensor(out=V(T0), in0=V(Q_R), in1=V(A_R), op=ALU.mult)
        v.tensor_tensor(out=V(T1), in0=V(Q_I), in1=V(A_I), op=ALU.mult)
        v.tensor_tensor(out=V(T2), in0=V(T0), in1=V(T1), op=ALU.add)
        v.tensor_tensor(out=V(T0), in0=V(Q_R), in1=V(A_I), op=ALU.mult)
        v.tensor_tensor(out=V(T1), in0=V(Q_I), in1=V(A_R), op=ALU.mult)
        v.tensor_tensor(out=V(T3), in0=V(T0), in1=V(T1), op=ALU.subtract)

    @chain("tensor", "b_sden_mm")
    def _(t):
        t.wait_ge(d_small, 16 * NSMALL)
        t.matmul(miscp[0:64, 1:2], MR[8][:, :], vec[:, BP_:BP_ + 1], start=True, stop=True)
        t.matmul(miscp[0:64, 2:3], MI[8][:, :], vec[:, BP_:BP_ + 1], start=True, stop=True)
        t.matmul(miscp[0:1, 3:4], ones_f[0:64, 0:1], vec[:, T2:T2 + 1], start=True, stop=True)
        t.matmul(miscp[0:1, 4:5], ones_f[0:64, 0:1], vec[:, T3:T3 + 1], start=True, stop=True)

    @chain("vector", "sden")
    def _(v):
        v.tensor_copy(out=V(B_R), in_=miscp[0:64, 1:2])
        v.tensor_copy(out=V(B_I), in_=miscp[0:64, 2:3])
        v.tensor_scalar(out=sc11[0:1, 0:1], in0=miscp[0:1, 3:4], scalar1=1.0,
                        scalar2=None, op0=ALU.add)
        v.tensor_copy(out=sc11[0:1, 1:2], in_=miscp[0:1, 4:5])
        v.tensor_tensor(out=sc11[0:1, 2:3], in0=sc11[0:1, 0:1], in1=sc11[0:1, 0:1], op=ALU.mult)
        v.tensor_tensor(out=sc11[0:1, 3:4], in0=sc11[0:1, 1:2], in1=sc11[0:1, 1:2], op=ALU.mult)
        v.tensor_tensor(out=sc11[0:1, 4:5], in0=sc11[0:1, 2:3], in1=sc11[0:1, 3:4], op=ALU.add)
        v.reciprocal(out=sc11[0:1, 5:6], in_=sc11[0:1, 4:5])
        v.tensor_tensor(out=sc11[0:1, 6:7], in0=sc11[0:1, 0:1], in1=sc11[0:1, 5:6], op=ALU.mult)
        v.tensor_tensor(out=sc11[0:1, 7:8], in0=sc11[0:1, 1:2], in1=sc11[0:1, 5:6], op=ALU.mult)
        v.tensor_scalar(out=sc11[0:1, 7:8], in0=sc11[0:1, 7:8], scalar1=-1.0,
                        scalar2=None, op0=ALU.mult)

    @chain("tensor", "sinv_bcast")
    def _(t):
        t.matmul(miscp[0:64, 5:6], onesr_f[0:1, 0:64], sc11[0:1, 6:7], start=True, stop=True)
        t.matmul(miscp[0:64, 6:7], onesr_f[0:1, 0:64], sc11[0:1, 7:8], start=True, stop=True)

    @chain("vector", "rb_el")
    def _(v):
        SI_R, SI_I = T4T5 = (W_R, W_I)
        v.tensor_copy(out=V(SI_R), in_=miscp[0:64, 5:6])
        v.tensor_copy(out=V(SI_I), in_=miscp[0:64, 6:7])
        cmul_vec(v, T2, T3, R_R, R_I, B_R, B_I)

    @chain("tensor", "rb_sum")
    def _(t):
        t.matmul(miscp[0:1, 7:8], ones_f[0:64, 0:1], vec[:, T2:T2 + 1], start=True, stop=True)
        t.matmul(miscp[0:1, 8:9], ones_f[0:64, 0:1], vec[:, T3:T3 + 1], start=True, stop=True)

    @chain("vector", "rb_scal")
    def _(v):
        v.tensor_copy(out=sc11[0:1, 0:1], in_=miscp[0:1, 7:8])
        v.tensor_copy(out=sc11[0:1, 1:2], in_=miscp[0:1, 8:9])
        v.tensor_tensor(out=sc11[0:1, 2:3], in0=sc11[0:1, 0:1], in1=sc11[0:1, 6:7], op=ALU.mult)
        v.tensor_tensor(out=sc11[0:1, 3:4], in0=sc11[0:1, 1:2], in1=sc11[0:1, 7:8], op=ALU.mult)
        v.tensor_tensor(out=sc11[0:1, 4:5], in0=sc11[0:1, 2:3], in1=sc11[0:1, 3:4], op=ALU.subtract)
        v.tensor_tensor(out=sc11[0:1, 2:3], in0=sc11[0:1, 0:1], in1=sc11[0:1, 7:8], op=ALU.mult)
        v.tensor_tensor(out=sc11[0:1, 3:4], in0=sc11[0:1, 1:2], in1=sc11[0:1, 6:7], op=ALU.mult)
        v.tensor_tensor(out=sc11[0:1, 5:6], in0=sc11[0:1, 2:3], in1=sc11[0:1, 3:4], op=ALU.add)

    @chain("tensor", "rb_bcast")
    def _(t):
        t.matmul(sparep[0:64, 0:1], onesr_f[0:1, 0:64], sc11[0:1, 4:5], start=True, stop=True)
        t.matmul(sparep[0:64, 1:2], onesr_f[0:1, 0:64], sc11[0:1, 5:6], start=True, stop=True)

    @chain("vector", "btil")
    def _(v):
        CR, CI = T4, T5 = T0, T1
        v.tensor_copy(out=V(T2), in_=sparep[0:64, 0:1])
        v.tensor_copy(out=V(T3), in_=sparep[0:64, 1:2])
        v.tensor_tensor(out=V(T0), in0=V(A_R), in1=V(T2), op=ALU.mult)
        v.tensor_tensor(out=V(T1), in0=V(A_I), in1=V(T3), op=ALU.mult)
        v.tensor_tensor(out=V(W_R), in0=V(T0), in1=V(T1), op=ALU.subtract)
        v.tensor_tensor(out=V(T0), in0=V(A_R), in1=V(T3), op=ALU.mult)
        v.tensor_tensor(out=V(T1), in0=V(A_I), in1=V(T2), op=ALU.mult)
        v.tensor_tensor(out=V(W_I), in0=V(T0), in1=V(T1), op=ALU.add)
        cmul_vec(v, T2, T3, DMI_R, DMI_I, B_R, B_I)
        v.tensor_tensor(out=V(BT_R), in0=V(T2), in1=V(W_R), op=ALU.subtract)
        v.tensor_tensor(out=V(BT_I), in0=V(T3), in1=V(W_I), op=ALU.subtract)
        cmul_vec(v, G_R, G_I, DMI_R, DMI_I, DP_R, DP_I)
        cmul_vec(v, T2, T3, R_R, R_I, U_R, U_I)

    @chain("tensor", "rho_sum")
    def _(t):
        t.matmul(sparep[0:1, 2:3], ones_f[0:64, 0:1], vec[:, T2:T2 + 1], start=True, stop=True)
        t.matmul(sparep[0:1, 3:4], ones_f[0:64, 0:1], vec[:, T3:T3 + 1], start=True, stop=True)

    @chain("vector", "kappa")
    def _(v):
        v.tensor_copy(out=sc11[0:1, 0:1], in_=sparep[0:1, 2:3])
        v.tensor_copy(out=sc11[0:1, 1:2], in_=sparep[0:1, 3:4])
        v.tensor_tensor(out=sc11[0:1, 2:3], in0=sc11[0:1, 0:1], in1=sc11[0:1, 6:7], op=ALU.mult)
        v.tensor_tensor(out=sc11[0:1, 3:4], in0=sc11[0:1, 1:2], in1=sc11[0:1, 7:8], op=ALU.mult)
        v.tensor_tensor(out=sc11[0:1, 4:5], in0=sc11[0:1, 2:3], in1=sc11[0:1, 3:4], op=ALU.subtract)
        v.tensor_scalar(out=sc11[0:1, 4:5], in0=sc11[0:1, 4:5], scalar1=-1.0,
                        scalar2=None, op0=ALU.add)
        v.tensor_tensor(out=sc11[0:1, 2:3], in0=sc11[0:1, 0:1], in1=sc11[0:1, 7:8], op=ALU.mult)
        v.tensor_tensor(out=sc11[0:1, 3:4], in0=sc11[0:1, 1:2], in1=sc11[0:1, 6:7], op=ALU.mult)
        v.tensor_tensor(out=sc11[0:1, 5:6], in0=sc11[0:1, 2:3], in1=sc11[0:1, 3:4], op=ALU.add)

    @chain("tensor", "kap_bcast")
    def _(t):
        t.matmul(sparep[0:64, 4:5], onesr_f[0:1, 0:64], sc11[0:1, 4:5], start=True, stop=True)
        t.matmul(sparep[0:64, 5:6], onesr_f[0:1, 0:64], sc11[0:1, 5:6], start=True, stop=True)

    @chain("vector", "mcol")
    def _(v):
        KR, KI = T4, T5 = (T2, T3)
        v.tensor_copy(out=V(T2), in_=sparep[0:64, 4:5])
        v.tensor_copy(out=V(T3), in_=sparep[0:64, 5:6])
        v.tensor_scalar(out=V(T0), in0=V(Q_I), scalar1=-1.0, scalar2=None, op0=ALU.mult)
        v.tensor_tensor(out=V(M_R), in0=V(T2), in1=V(Q_R), op=ALU.mult)
        v.tensor_tensor(out=V(T1), in0=V(T3), in1=V(T0), op=ALU.mult)
        v.tensor_tensor(out=V(M_R), in0=V(M_R), in1=V(T1), op=ALU.subtract)
        v.tensor_tensor(out=V(M_I), in0=V(T2), in1=V(T0), op=ALU.mult)
        v.tensor_tensor(out=V(T1), in0=V(T3), in1=V(Q_R), op=ALU.mult)
        v.tensor_tensor(out=V(M_I), in0=V(M_I), in1=V(T1), op=ALU.add)
        cmul_vec(v, T2, T3, R_R, R_I, DP_R, DP_I)
        v.tensor_copy(out=V(W_R), in_=miscp[0:64, 5:6])
        v.tensor_copy(out=V(W_I), in_=miscp[0:64, 6:7])
        v.tensor_tensor(out=V(T0), in0=V(W_R), in1=V(T2), op=ALU.mult)
        v.tensor_tensor(out=V(T1), in0=V(W_I), in1=V(T3), op=ALU.mult)
        v.tensor_tensor(out=V(T0), in0=V(T0), in1=V(T1), op=ALU.subtract)
        v.tensor_tensor(out=V(M_R), in0=V(M_R), in1=V(T0), op=ALU.subtract)
        v.tensor_tensor(out=V(T0), in0=V(W_R), in1=V(T3), op=ALU.mult)
        v.tensor_tensor(out=V(T1), in0=V(W_I), in1=V(T2), op=ALU.mult)
        v.tensor_tensor(out=V(T0), in0=V(T0), in1=V(T1), op=ALU.add)
        v.tensor_tensor(out=V(M_I), in0=V(M_I), in1=V(T0), op=ALU.subtract)

    @chain("tensor", "arow_t")
    def _(t):
        t.transpose(sparep[0:1, 64:128], vec[:, A_R:A_R + 1], i64f[:, :])
        t.transpose(sparep[0:1, 128:192], vec[:, A_I:A_I + 1], i64f[:, :])
        t.transpose(sparep[0:1, 192:256], vec[:, M_R:M_R + 1], i64f[:, :])
        t.transpose(sparep[0:1, 256:320], vec[:, M_I:M_I + 1], i64f[:, :])

    @chain("vector", "arow_cp", serial=False)
    def _(v):
        v.tensor_copy(out=arow[0:1, 0:64], in_=sparep[0:1, 64:128])
        v.tensor_copy(out=arow[0:1, 64:128], in_=sparep[0:1, 128:192])
        v.tensor_copy(out=mrow[0:1, 0:64], in_=sparep[0:1, 192:256])
        v.tensor_copy(out=mrow[0:1, 64:128], in_=sparep[0:1, 256:320])
        v.tensor_scalar(out=arow[0:1, 128:192], in0=sparep[0:1, 128:192],
                        scalar1=-1.0, scalar2=None, op0=ALU.mult)
        v.tensor_scalar(out=arow[0:1, 192:256], in0=sparep[0:1, 256:320],
                        scalar1=-1.0, scalar2=None, op0=ALU.mult)

    @chain("tensor", "atil_mm")
    def _(t):
        t.matmul(miscp[0:64, 0:64], arow[0:1, 0:64], mrow[0:1, 0:64], start=True, stop=False)
        t.matmul(miscp[0:64, 0:64], arow[0:1, 128:192], mrow[0:1, 64:128], start=False, stop=True)
        t.matmul(miscp[0:64, 64:128], arow[0:1, 0:64], mrow[0:1, 64:128], start=True, stop=False)
        t.matmul(miscp[0:64, 64:128], arow[0:1, 64:128], mrow[0:1, 0:64], start=False, stop=True)
        t.matmul(miscp[0:64, 128:192], mrow[0:1, 0:64], arow[0:1, 0:64], start=True, stop=False)
        t.matmul(miscp[0:64, 128:192], arow[0:1, 192:256], arow[0:1, 64:128], start=False, stop=True)
        t.matmul(miscp[0:64, 192:256], mrow[0:1, 0:64], arow[0:1, 64:128], start=True, stop=False)
        t.matmul(miscp[0:64, 192:256], mrow[0:1, 64:128], arow[0:1, 0:64], start=False, stop=True)

    @chain("vector", "atil", serial=False)
    def _(v):
        v.tensor_copy(out=MR[0][:, :], in_=miscp[0:64, 0:64])
        v.tensor_copy(out=MI[0][:, :], in_=miscp[0:64, 64:128])
        v.tensor_copy(out=MTR[0][:, :], in_=miscp[0:64, 128:192])
        v.tensor_copy(out=MTI[0][:, :], in_=miscp[0:64, 192:256])
        v.tensor_scalar(out=Tk32[0:64, 0:64], in0=i64f[:, :],
                        scalar1=vec[0:64, G_R:G_R + 1], scalar2=None, op0=ALU.mult)
        v.tensor_scalar(out=Tk32[0:64, 64:128], in0=i64f[:, :],
                        scalar1=vec[0:64, G_I:G_I + 1], scalar2=None, op0=ALU.mult)
        v.tensor_tensor(out=MR[0][:, :], in0=MR[0][:, :], in1=Tk32[0:64, 0:64], op=ALU.add)
        v.tensor_tensor(out=MI[0][:, :], in0=MI[0][:, :], in1=Tk32[0:64, 64:128], op=ALU.add)
        v.tensor_tensor(out=MTR[0][:, :], in0=MTR[0][:, :], in1=Tk32[0:64, 0:64], op=ALU.add)
        v.tensor_tensor(out=MTI[0][:, :], in0=MTI[0][:, :], in1=Tk32[0:64, 64:128], op=ALU.add)
        v.tensor_scalar(out=negscr[:, :], in0=MTI[0][:, :], scalar1=-1.0, scalar2=None, op0=ALU.mult)
        v.tensor_scalar(out=negscr2[:, :], in0=MI[0][:, :], scalar1=-1.0, scalar2=None, op0=ALU.mult)

    def sq_mm(t, k):
        t.matmul(miscp[0:64, 0:64], MTR[k][:, :], MR[k][:, :], start=True, stop=False)
        t.matmul(miscp[0:64, 0:64], negscr[:, :], MI[k][:, :], start=False, stop=True)
        t.matmul(miscp[0:64, 64:128], MTR[k][:, :], MI[k][:, :], start=True, stop=False)
        t.matmul(miscp[0:64, 64:128], MTI[k][:, :], MR[k][:, :], start=False, stop=True)
        t.matmul(miscp[0:64, 128:192], MR[k][:, :], MTR[k][:, :], start=True, stop=False)
        t.matmul(miscp[0:64, 128:192], negscr2[:, :], MTI[k][:, :], start=False, stop=True)
        t.matmul(miscp[0:64, 192:256], MR[k][:, :], MTI[k][:, :], start=True, stop=False)
        t.matmul(miscp[0:64, 192:256], MI[k][:, :], MTR[k][:, :], start=False, stop=True)

    def sq_evac(v, kk, extra=None):
        v.tensor_copy(out=MI[kk][:, :], in_=miscp[0:64, 64:128])
        v.tensor_copy(out=MTI[kk][:, :], in_=miscp[0:64, 192:256])
        v.tensor_copy(out=MR[kk][:, :], in_=miscp[0:64, 0:64])
        v.tensor_copy(out=MTR[kk][:, :], in_=miscp[0:64, 128:192])
        v.tensor_scalar(out=negscr2[:, :], in0=MI[kk][:, :], scalar1=-1.0, scalar2=None, op0=ALU.mult)
        v.tensor_scalar(out=negscr[:, :], in0=MTI[kk][:, :], scalar1=-1.0, scalar2=None, op0=ALU.mult)
        if extra:
            extra(v)

    def mk_square(k, kk, extra=None):
        @chain("tensor")
        def _(t):
            sq_mm(t, k)

        @chain("vector", serial=False)
        def _(v):
            sq_evac(v, kk, extra)

    def _negm7(v):
        v.tensor_scalar(out=gt[0:64, 0:64], in0=MTI[7][:, :], scalar1=-1.0,
                        scalar2=None, op0=ALU.mult)

    for k in range(6):
        mk_square(k, k + 1)
    mk_square(6, 7, extra=_negm7)
    mk_square(7, 8)
    mk_square(8, 9)
    mk_square(9, 8)
    mk_square(8, 9)

    def _winit(v):
        v.tensor_copy(out=V(W_R), in_=V(CT_R))
        v.tensor_copy(out=V(W_I), in_=V(CT_I))
    mk_square(9, 8, extra=_winit)

    def vprod_step(slot, do_square):
        @chain("tensor")
        def _(t):
            t.matmul(miscp[0:64, 320:321], MR[slot][:, :], vec[:, W_R:W_R + 1], start=True, stop=False)
            t.matmul(miscp[0:64, 320:321], MI[slot][:, :], vec[:, W_I:W_I + 1], start=False, stop=True)
            t.matmul(miscp[0:64, 321:322], MR[slot][:, :], vec[:, W_I:W_I + 1], start=True, stop=False)
            t.matmul(miscp[0:64, 321:322], negscr2[:, :], vec[:, W_R:W_R + 1], start=False, stop=True)
            if do_square:
                sq_mm(t, slot)

        @chain("vector", serial=False)
        def _(v):
            v.tensor_tensor(out=V(W_R), in0=V(W_R), in1=miscp[0:64, 320:321], op=ALU.add)
            v.tensor_tensor(out=V(W_I), in0=V(W_I), in1=miscp[0:64, 321:322], op=ALU.add)
            if do_square:
                sq_evac(v, 9 if slot == 8 else 8)

    vprod_step(8, True)
    vprod_step(9, True)
    vprod_step(8, True)
    vprod_step(9, False)

    @chain("vector", "v_final")
    def _(v):
        v.tensor_tensor(out=V(V_R), in0=V(STP64), in1=V(W_R), op=ALU.mult)
        v.tensor_tensor(out=V(V_I), in0=V(STP64), in1=V(W_I), op=ALU.mult)
        v.tensor_copy(out=Pr[:, 127:128], in_=V(BT_R))
        v.tensor_copy(out=Pi[:, 127:128], in_=V(BT_I))

    @chain("vector", "w0neg")
    def _(v):
        v.tensor_scalar(out=negscr2[:, :], in0=MI[0][:, :], scalar1=-1.0, scalar2=None, op0=ALU.mult)

    @chain("tensor", "w0mm")
    def _(t):
        t.matmul(miscp[0:64, 320:321], MR[0][:, :], vec[:, V_R:V_R + 1], start=True, stop=False)
        t.matmul(miscp[0:64, 320:321], MI[0][:, :], vec[:, V_I:V_I + 1], start=False, stop=True)
        t.matmul(miscp[0:64, 321:322], MR[0][:, :], vec[:, V_I:V_I + 1], start=True, stop=False)
        t.matmul(miscp[0:64, 321:322], negscr2[:, :], vec[:, V_R:V_R + 1], start=False, stop=True)

    @chain("vector", "w0cp", serial=False)
    def _(v):
        v.tensor_copy(out=Wr[:, 0:1], in_=miscp[0:64, 320:321])
        v.tensor_copy(out=Wi[:, 0:1], in_=miscp[0:64, 321:322])

    def pw_double(k):
        n = 1 << k
        lo = P - 2 * n

        @chain("vector", serial=False)
        def _(v):
            v.tensor_scalar(out=negscr[:, :], in0=MTI[k][:, :], scalar1=-1.0, scalar2=None, op0=ALU.mult)
            v.tensor_scalar(out=negscr2[:, :], in0=MI[k][:, :], scalar1=-1.0, scalar2=None, op0=ALU.mult)

        @chain("tensor")
        def _(t):
            t.matmul(miscp[0:64, 0:n], MTR[k][:, :], Pr[:, P - n:P], start=True, stop=False)
            t.matmul(miscp[0:64, 0:n], negscr[:, :], Pi[:, P - n:P], start=False, stop=True)
            t.matmul(miscp[0:64, 128:128 + n], MTR[k][:, :], Pi[:, P - n:P], start=True, stop=False)
            t.matmul(miscp[0:64, 128:128 + n], MTI[k][:, :], Pr[:, P - n:P], start=False, stop=True)
            t.matmul(miscp[0:64, 256:256 + n], MR[k][:, :], Wr[:, 0:n], start=True, stop=False)
            t.matmul(miscp[0:64, 256:256 + n], MI[k][:, :], Wi[:, 0:n], start=False, stop=True)
            t.matmul(miscp[0:64, 384:384 + n], MR[k][:, :], Wi[:, 0:n], start=True, stop=False)
            t.matmul(miscp[0:64, 384:384 + n], negscr2[:, :], Wr[:, 0:n], start=False, stop=True)

        @chain("vector", serial=False)
        def _(v):
            v.tensor_copy(out=Pr[:, lo:lo + n], in_=miscp[0:64, 0:n])
            v.tensor_copy(out=Pi[:, lo:lo + n], in_=miscp[0:64, 128:128 + n])
            v.tensor_copy(out=Wr[:, n:2 * n], in_=miscp[0:64, 256:256 + n])
            v.tensor_copy(out=Wi[:, n:2 * n], in_=miscp[0:64, 384:384 + n])

    for k in range(7):
        pw_double(k)

    @chain("vector", "k0_el")
    def _(v):
        v.tensor_tensor(out=V(T0), in0=V(V_R), in1=V(BT_R), op=ALU.mult)
        v.tensor_tensor(out=V(T1), in0=V(V_I), in1=V(BT_I), op=ALU.mult)
        v.tensor_tensor(out=V(T2), in0=V(T0), in1=V(T1), op=ALU.add)

    @chain("tensor", "kvec_mm")
    def _(t):
        t.matmul(miscp[:, 0:1], Wr[:, :], vec[:, BT_R:BT_R + 1], start=True, stop=False)
        t.matmul(miscp[:, 0:1], Wi[:, :], vec[:, BT_I:BT_I + 1], start=False, stop=True)
        t.matmul(miscp[0:1, 2:3], ones_f[0:64, 0:1], vec[:, T2:T2 + 1], start=True, stop=True)

    @chain("vector", "kcol_cp")
    def _(v):
        v.tensor_copy(out=kcol[:, 0:1], in_=miscp[:, 0:1])
        v.tensor_tensor(out=Krow[0:1, 0:1], in0=miscp[0:1, 2:3], in1=dD1[0:1, 0:1], op=ALU.add)

    @chain("tensor", "krow_mm")
    def _(t):
        t.transpose(miscp[0:1, 4:4 + P], kcol[:, 0:1], i128f[:, :])

    @chain("vector", "krow")
    def _(v):
        v.tensor_copy(out=Krow[0:1, 1:P], in_=miscp[0:1, 4:4 + 127])
        v.memset(Tkb[:, :], 0.0)
        v.tensor_copy(out=Tkb[0:1, 0:P], in_=Krow[0:1, 0:P])

    @chain("tensor", "pq_t")
    def _(t):
        t.transpose(miscp[0:P, 0:64], Pr[:, :], i64f[:, :])
        t.transpose(miscp[0:P, 64:128], Pi[:, :], i64f[:, :])

    @chain("vector", "pbig")
    def _(v):
        v.tensor_copy(out=Pbig[:, 0:64], in_=miscp[0:P, 0:64])
        v.tensor_copy(out=Pbig[:, 64:128], in_=miscp[0:P, 64:128])

    @free("gpsimd", name="qm_dma")
    def _(g):
        g.wait_ge(ch, CH_IDX["k0_el"])
        pairs = [
            (Qbig[0:64, :], Wr[:, :]),
            (Qbig[64:128, :], Wi[:, :]),
            (Tk32[0:64, 0:64], MTR[7][:, :]),
            (Tk32[64:128, 64:128], MTR[7][:, :]),
            (Tk32[0:64, 64:128], MTI[7][:, :]),
            (Tk32[64:128, 0:64], gt[0:64, 0:64]),
        ]
        for dst, src in pairs:
            g.dma_start(out=dst, in_=src).then_inc(d_tk, 16)

    @free("gpsimd", name="tk_double")
    def _(g):
        g.wait_ge(ch, CH_IDX["krow"] + 1)
        for k in range(7):
            s0 = 1 << k
            g.wait_ge(d_tk, 96 + 16 * k)
            g.dma_start(out=Tkb[s0:2 * s0, s0:P], in_=Tkb[0:s0, 0:P - s0]).then_inc(d_tk, 16)

    @chain("vector", "qm_cp")
    def _(v):
        v.wait_ge(d_tk, 96)
        v.tensor_copy(out=Mhi[:, :], in_=Tk32[:, :])
        v.drain()
        v.tensor_tensor(out=Mlo[:, :], in0=Tk32[:, :], in1=Mhi[:, :], op=ALU.subtract)

    if DBG:
        @chain("gpsimd", "dbg_early")
        def _(g):
            pairs = [
                (d_dbg.ap()[:, 6100:6228], Mhi[:, :]),
                (d_dbg.ap()[:, 6228:6356], Qbig[:, :]),
                (d_dbg.ap()[0:64, 6356:6484], selhi[:, :]),
                (d_dbg.ap()[0:64, 6484:6548], MTR[7][:, :]),
                (d_dbg.ap()[0:64, 6548:6612], MTI[7][:, :]),
                (d_dbg.ap()[0:64, 6612:6676], negscr[:, :]),
                (d_dbg.ap()[0:64, 6676:6740], i64f[:, :]),
                (d_dbg.ap()[:, 6740:6868], Pbig[:, :]),
                (d_dbg.ap()[:, 6868:6996], Mlo[:, :]),
                (d_dbg.ap()[0:64, 6996:7060], MR[0][:, :]),
                (d_dbg.ap()[0:64, 7060:7124], MI[0][:, :]),
                (d_dbg.ap()[0:64, 7124:7188], MTR[0][:, :]),
                (d_dbg.ap()[0:64, 7188:7252], MR[1][:, :]),
                (d_dbg.ap()[0:64, 7252:7316], MTR[1][:, :]),
                (d_dbg.ap()[0:1, 7316:7572], arow[0:1, :]),
                (d_dbg.ap()[0:1, 7572:7700], mrow[0:1, :]),
                (d_dbg.ap()[0:64, 7700:7740], vec[:, :]),
                (d_dbg.ap()[0:64, 7740:7868], Wr[:, :]),
                (d_dbg.ap()[0:64, 7868:7996], Pr[:, :]),
            ]
            n = 0
            for dst, src in pairs:
                g.dma_start(out=dst, in_=src, allow_slow_non_contiguous=True).then_inc(d_tk, 16)
                n += 1
            g.wait_ge(d_tk, 16 * (7 + n))

    @chain("vector", "setup_end")
    def _(v):
        pass

    def mk_stat_unit(u):
        s_ = u % NS
        rnd = u // NS

        @free("tensor", name=f"stu{u}")
        def _(t):
            if u == NS:
                t.wait_ge(ch, CH_IDX["rowsA"] + 1)
            t.wait_ge(d_xbf, 16 * (s_ + 1))
            t.wait_ge(v_sq, u + 1)
            buf = tmp_sc if u % 2 == 0 else tmp_bf
            ta, tb = (0, 1) if rnd == 0 else (2, 3)
            t.matmul(ob[0][0:1, :], ones_bf[:, 0:1], xn_ap(s_, ta * TS, TS),
                     start=(s_ == 0), stop=(s_ == NS - 1))
            t.matmul(ob[1][0:1, :], ones_bf[:, 0:1], buf[:, ta * TS:(ta + 1) * TS],
                     start=(s_ == 0), stop=(s_ == NS - 1))
            t.matmul(ob[2][0:1, :], ones_bf[:, 0:1], xn_ap(s_, tb * TS, TS),
                     start=(s_ == 0), stop=(s_ == NS - 1))
            t.matmul(ob[3][0:1, :], ones_bf[:, 0:1], buf[:, tb * TS:(tb + 1) * TS],
                     start=(s_ == 0), stop=(s_ == NS - 1)).then_inc(t_st, 1)

    for u in range(2 * NS):
        mk_stat_unit(u)

    def mk_squares(rnd):
        @free("scalar", name=f"sq{rnd}")
        def _(sc):
            for uu in range(NS):
                u = rnd * NS + uu
                s_ = u % NS
                sc.wait_ge(d_xbf, 16 * (s_ + 1))
                if u >= 2:
                    sc.wait_ge(t_st, u - 1)
                buf = tmp_sc if u % 2 == 0 else tmp_bf
                sc.activation(out=buf[:, :], in_=actbuf[:, s_ * T:(s_ + 1) * T],
                              func=AF.Square).then_inc(v_sq, 1)

    mk_squares(0)
    mk_squares(1)

    @chain("vector", "rowsA", serial=False)
    def _(v):
        v.wait_ge(t_st, NS)
        v.tensor_scalar(out=scr_a[0:1, 0:TS], in0=ob[0][0:1, :],
                        scalar1=1.0 / DM, scalar2=None, op0=ALU.mult)
        v.tensor_scalar(out=scr_a[0:1, 2 * TS:3 * TS], in0=ob[2][0:1, :],
                        scalar1=1.0 / DM, scalar2=None, op0=ALU.mult)
        v.tensor_scalar(out=scr_a[0:1, TS:2 * TS], in0=ob[1][0:1, :],
                        scalar1=1.0 / DM, scalar2=None, op0=ALU.mult)
        v.tensor_scalar(out=scr_a[0:1, 3 * TS:4 * TS], in0=ob[3][0:1, :],
                        scalar1=1.0 / DM, scalar2=None, op0=ALU.mult)
        v.drain()
        v.tensor_tensor(out=rowscr[0:1, 0:TS], in0=scr_a[0:1, 0:TS],
                        in1=scr_a[0:1, 0:TS], op=ALU.mult)
        v.tensor_tensor(out=rowscr[0:1, TS:2 * TS], in0=scr_a[0:1, 2 * TS:3 * TS],
                        in1=scr_a[0:1, 2 * TS:3 * TS], op=ALU.mult)
        v.drain()
        v.tensor_tensor(out=scr_a[0:1, TS:2 * TS], in0=scr_a[0:1, TS:2 * TS],
                        in1=rowscr[0:1, 0:TS], op=ALU.subtract)
        v.tensor_tensor(out=scr_a[0:1, 3 * TS:4 * TS], in0=scr_a[0:1, 3 * TS:4 * TS],
                        in1=rowscr[0:1, TS:2 * TS], op=ALU.subtract)
        v.drain()
        v.tensor_scalar(out=scr_a[0:1, TS:2 * TS], in0=scr_a[0:1, TS:2 * TS],
                        scalar1=1e-5, scalar2=None, op0=ALU.add)
        v.tensor_scalar(out=scr_a[0:1, 3 * TS:4 * TS], in0=scr_a[0:1, 3 * TS:4 * TS],
                        scalar1=1e-5, scalar2=None, op0=ALU.add)

    @chain("scalar", "lnA", serial=False)
    def _(sc):
        sc.activation(out=rowscr[0:1, 0:TS], in_=scr_a[0:1, TS:2 * TS], func=AF.Ln)
        sc.activation(out=rowscr[0:1, TS:2 * TS], in_=scr_a[0:1, 3 * TS:4 * TS], func=AF.Ln)
        sc.drain()
        sc.activation(out=scr_a[0:1, TS:2 * TS], in_=rowscr[0:1, 0:TS], func=AF.Exp, scale=-0.5)
        sc.activation(out=scr_a[0:1, 3 * TS:4 * TS], in_=rowscr[0:1, TS:2 * TS], func=AF.Exp, scale=-0.5)

    @chain("vector", "mursA", serial=False)
    def _(v):
        v.tensor_tensor(out=rowscr[0:1, 0:TS], in0=scr_a[0:1, 0:TS],
                        in1=scr_a[0:1, TS:2 * TS], op=ALU.mult)
        v.tensor_tensor(out=rowscr[0:1, TS:2 * TS], in0=scr_a[0:1, 2 * TS:3 * TS],
                        in1=scr_a[0:1, 3 * TS:4 * TS], op=ALU.mult)

    @chain("vector", "rowsB", serial=False)
    def _(v):
        v.wait_ge(t_st, 2 * NS)
        v.tensor_scalar(out=tmp_sc[0:1, 2 * TS:3 * TS], in0=ob[0][0:1, :],
                        scalar1=1.0 / DM, scalar2=None, op0=ALU.mult)
        v.tensor_scalar(out=tmp_sc[0:1, 3 * TS:4 * TS], in0=ob[2][0:1, :],
                        scalar1=1.0 / DM, scalar2=None, op0=ALU.mult)
        v.tensor_scalar(out=tmp_sc[0:1, 0:TS], in0=ob[1][0:1, :],
                        scalar1=1.0 / DM, scalar2=None, op0=ALU.mult)
        v.tensor_scalar(out=tmp_sc[0:1, TS:2 * TS], in0=ob[3][0:1, :],
                        scalar1=1.0 / DM, scalar2=None, op0=ALU.mult)
        v.drain()
        v.tensor_tensor(out=tmp_bf[0:1, 0:TS], in0=tmp_sc[0:1, 2 * TS:3 * TS],
                        in1=tmp_sc[0:1, 2 * TS:3 * TS], op=ALU.mult)
        v.tensor_tensor(out=tmp_bf[0:1, TS:2 * TS], in0=tmp_sc[0:1, 3 * TS:4 * TS],
                        in1=tmp_sc[0:1, 3 * TS:4 * TS], op=ALU.mult)
        v.drain()
        v.tensor_tensor(out=tmp_sc[0:1, 0:TS], in0=tmp_sc[0:1, 0:TS],
                        in1=tmp_bf[0:1, 0:TS], op=ALU.subtract)
        v.tensor_tensor(out=tmp_sc[0:1, TS:2 * TS], in0=tmp_sc[0:1, TS:2 * TS],
                        in1=tmp_bf[0:1, TS:2 * TS], op=ALU.subtract)
        v.drain()
        v.tensor_scalar(out=tmp_sc[0:1, 0:TS], in0=tmp_sc[0:1, 0:TS],
                        scalar1=1e-5, scalar2=None, op0=ALU.add)
        v.tensor_scalar(out=tmp_sc[0:1, TS:2 * TS], in0=tmp_sc[0:1, TS:2 * TS],
                        scalar1=1e-5, scalar2=None, op0=ALU.add)

    @chain("scalar", "lnB", serial=False)
    def _(sc):
        sc.activation(out=tmp_bf[0:1, 0:TS], in_=tmp_sc[0:1, 0:TS], func=AF.Ln)
        sc.activation(out=tmp_bf[0:1, TS:2 * TS], in_=tmp_sc[0:1, TS:2 * TS], func=AF.Ln)
        sc.drain()
        sc.activation(out=scr_a[0:1, 0:TS], in_=tmp_bf[0:1, 0:TS], func=AF.Exp, scale=-0.5)
        sc.activation(out=scr_a[0:1, 2 * TS:3 * TS], in_=tmp_bf[0:1, TS:2 * TS], func=AF.Exp, scale=-0.5)

    @chain("vector", "mursB", serial=False)
    def _(v):
        v.tensor_tensor(out=tmp_bf[0:1, 0:TS], in0=tmp_sc[0:1, 2 * TS:3 * TS],
                        in1=scr_a[0:1, 0:TS], op=ALU.mult)
        v.tensor_tensor(out=tmp_bf[0:1, TS:2 * TS], in0=tmp_sc[0:1, 3 * TS:4 * TS],
                        in1=scr_a[0:1, 2 * TS:3 * TS], op=ALU.mult)

    @chain("tensor", "mapmmA")
    def _(t):
        t.matmul(ob[0][:, :], onesr_bf[0:1, 0:P], rowscr[0:1, 0:TS], start=True, stop=True)
        t.matmul(ob[1][:, :], onesr_f[0:1, 0:P], scr_a[0:1, TS:2 * TS], start=True, stop=True)
        t.matmul(ob[2][:, :], onesr_bf[0:1, 0:P], rowscr[0:1, TS:2 * TS], start=True, stop=True)
        t.matmul(ob[3][:, :], onesr_f[0:1, 0:P], scr_a[0:1, 3 * TS:4 * TS], start=True, stop=True)

    @chain("vector", "mapcpA", serial=False)
    def _(v):
        v.tensor_copy(out=map_mu[:, 0:TS], in_=ob[0][:, :])
        v.tensor_copy(out=map_rs[:, 0:TS], in_=ob[1][:, :])
        v.tensor_copy(out=map_mu[:, TS:2 * TS], in_=ob[2][:, :])
        v.tensor_copy(out=map_rs[:, TS:2 * TS], in_=ob[3][:, :])

    @chain("tensor", "mapmmB")
    def _(t):
        t.matmul(ob[0][:, :], onesr_bf[0:1, 0:P], tmp_bf[0:1, 0:TS], start=True, stop=True)
        t.matmul(ob[1][:, :], onesr_f[0:1, 0:P], scr_a[0:1, 0:TS], start=True, stop=True)
        t.matmul(ob[2][:, :], onesr_bf[0:1, 0:P], tmp_bf[0:1, TS:2 * TS], start=True, stop=True)
        t.matmul(ob[3][:, :], onesr_f[0:1, 0:P], scr_a[0:1, 2 * TS:3 * TS], start=True, stop=True)

    @chain("vector", "mapcpB", serial=False)
    def _(v):
        v.tensor_copy(out=map_mu[:, 2 * TS:3 * TS], in_=ob[0][:, :])
        v.tensor_copy(out=map_rs[:, 2 * TS:3 * TS], in_=ob[1][:, :])
        v.tensor_copy(out=map_mu[:, 3 * TS:4 * TS], in_=ob[2][:, :])
        v.tensor_copy(out=map_rs[:, 3 * TS:4 * TS], in_=ob[3][:, :])

    @free("vector")
    def _(v):
        for s in range(NS):
            v.tensor_tensor(out=tmp_sc[:, :], in0=actbuf[:, s * T:(s + 1) * T],
                            in1=map_rs[:, :], op=ALU.mult)
            v.tensor_tensor(out=actbuf[:, s * T:(s + 1) * T], in0=tmp_sc[:, :],
                            in1=map_mu[:, :], op=ALU.subtract).then_inc(v_xn, 1)

    @free("tensor")
    def _(t):
        t.wait_ge(d_gsm, 48)
        for s in range(NS):
            t.wait_ge(v_xn, s + 1)
            for tt in range(NT):
                t.matmul(ob[tt][:, :], lAx[:, s * P:(s + 1) * P],
                         xn_ap(s, tt * TS, TS), start=(s == 0), stop=(s == NS - 1))
            for tt in range(NT):
                t.matmul(hbanks[tt][0:32, :], g1b[:, s * 32:(s + 1) * 32],
                         xn_ap(s, tt * TS, TS), start=(s == 0), stop=(s == NS - 1))

    @chain("tensor", "iter0_done")
    def _(t):
        pass

    @chain("vector", "tmp_evac", serial=False)
    def _(v):
        for tt in range(NT):
            v.tensor_copy(out=tmp_bf[:, tt * TS:(tt + 1) * TS], in_=ob[tt][:, :])

    @chain("vector", "h_evac", serial=False)
    def _(v):
        for tt in range(NT):
            v.tensor_scalar(out=gh[0:32, tt * TS:(tt + 1) * TS], in0=hbanks[tt][0:32, :],
                            scalar1=gb1s[:, 0:1], scalar2=None, op0=ALU.add)

    @chain("vector", "hsq")
    def _(v):
        v.tensor_tensor(out=tmp_sc[0:32, :], in0=gh[0:32, :], in1=gh[0:32, :], op=ALU.mult)

    @chain("tensor", "hstat_mm")
    def _(t):
        for tt in range(NT):
            t.matmul(hbanks[tt][0:1, :], ones_bf[0:32, 0:1],
                     gh[0:32, tt * TS:(tt + 1) * TS], start=True, stop=True)
        for tt in range(NT):
            t.matmul(ob[tt][0:1, :], ones_bf[0:32, 0:1],
                     tmp_sc[0:32, tt * TS:(tt + 1) * TS], start=True, stop=True)

    @chain("vector", "hrows_r0", serial=False)
    def _(v):
        v.tensor_scalar(out=scr_a[0:1, 0:TS], in0=hbanks[0][0:1, :],
                        scalar1=1.0 / 32, scalar2=None, op0=ALU.mult)
        v.tensor_scalar(out=scr_a[0:1, 2 * TS:3 * TS], in0=hbanks[1][0:1, :],
                        scalar1=1.0 / 32, scalar2=None, op0=ALU.mult)
        v.tensor_scalar(out=scr_a[0:1, TS:2 * TS], in0=ob[0][0:1, :],
                        scalar1=1.0 / 32, scalar2=None, op0=ALU.mult)
        v.tensor_scalar(out=scr_a[0:1, 3 * TS:4 * TS], in0=ob[1][0:1, :],
                        scalar1=1.0 / 32, scalar2=None, op0=ALU.mult)
        v.drain()
        v.tensor_tensor(out=rowscr[0:1, 0:TS], in0=scr_a[0:1, 0:TS],
                        in1=scr_a[0:1, 0:TS], op=ALU.mult)
        v.tensor_tensor(out=rowscr[0:1, TS:2 * TS], in0=scr_a[0:1, 2 * TS:3 * TS],
                        in1=scr_a[0:1, 2 * TS:3 * TS], op=ALU.mult)
        v.drain()
        v.tensor_tensor(out=scr_a[0:1, TS:2 * TS], in0=scr_a[0:1, TS:2 * TS],
                        in1=rowscr[0:1, 0:TS], op=ALU.subtract)
        v.tensor_tensor(out=scr_a[0:1, 3 * TS:4 * TS], in0=scr_a[0:1, 3 * TS:4 * TS],
                        in1=rowscr[0:1, TS:2 * TS], op=ALU.subtract)
        v.drain()
        v.tensor_scalar(out=scr_a[0:1, TS:2 * TS], in0=scr_a[0:1, TS:2 * TS],
                        scalar1=1e-5, scalar2=None, op0=ALU.add)
        v.tensor_scalar(out=scr_a[0:1, 3 * TS:4 * TS], in0=scr_a[0:1, 3 * TS:4 * TS],
                        scalar1=1e-5, scalar2=None, op0=ALU.add)

    @chain("scalar", "hln_r0", serial=False)
    def _(s):
        s.activation(out=rowscr[0:1, 0:TS], in_=scr_a[0:1, TS:2 * TS], func=AF.Ln)
        s.activation(out=rowscr[0:1, TS:2 * TS], in_=scr_a[0:1, 3 * TS:4 * TS], func=AF.Ln)
        s.drain()
        s.activation(out=scr_a[0:1, TS:2 * TS], in_=rowscr[0:1, 0:TS], func=AF.Exp, scale=-0.5)
        s.activation(out=scr_a[0:1, 3 * TS:4 * TS], in_=rowscr[0:1, TS:2 * TS], func=AF.Exp, scale=-0.5)

    @chain("tensor", "hmap_r0")
    def _(t):
        t.matmul(hbanks[0][0:32, :], onesr_f[0:1, 0:32], scr_a[0:1, 0:TS], start=True, stop=True)
        t.matmul(ob[0][0:32, :], onesr_f[0:1, 0:32], scr_a[0:1, TS:2 * TS], start=True, stop=True)
        t.matmul(hbanks[1][0:32, :], onesr_f[0:1, 0:32], scr_a[0:1, 2 * TS:3 * TS], start=True, stop=True)
        t.matmul(ob[1][0:32, :], onesr_f[0:1, 0:32], scr_a[0:1, 3 * TS:4 * TS], start=True, stop=True)

    @chain("vector", "hlnv_r0", serial=False)
    def _(v):
        for tt in (0, 1):
            v.tensor_tensor(out=scr_a[0:32, tt * TS:(tt + 1) * TS],
                            in0=gh[0:32, tt * TS:(tt + 1) * TS],
                            in1=hbanks[tt][0:32, :], op=ALU.subtract)
        v.drain()
        for tt in (0, 1):
            v.tensor_tensor(out=scr_a[0:32, tt * TS:(tt + 1) * TS],
                            in0=scr_a[0:32, tt * TS:(tt + 1) * TS],
                            in1=ob[tt][0:32, :], op=ALU.mult)

    @chain("scalar", "hrelu_r0")
    def _(s):
        s.activation(out=gh[0:32, 0 * TS:(1 + 1) * TS], in_=scr_a[0:32, 0 * TS:(1 + 1) * TS],
                     func=AF.Relu, bias=lnbs[:, 0:1], scale=lngs[:, 0:1])

    @chain("vector", "hrows_r1", serial=False)
    def _(v):
        v.tensor_scalar(out=scr_a[0:1, 0:TS], in0=hbanks[2][0:1, :],
                        scalar1=1.0 / 32, scalar2=None, op0=ALU.mult)
        v.tensor_scalar(out=scr_a[0:1, 2 * TS:3 * TS], in0=hbanks[3][0:1, :],
                        scalar1=1.0 / 32, scalar2=None, op0=ALU.mult)
        v.tensor_scalar(out=scr_a[0:1, TS:2 * TS], in0=ob[2][0:1, :],
                        scalar1=1.0 / 32, scalar2=None, op0=ALU.mult)
        v.tensor_scalar(out=scr_a[0:1, 3 * TS:4 * TS], in0=ob[3][0:1, :],
                        scalar1=1.0 / 32, scalar2=None, op0=ALU.mult)
        v.drain()
        v.tensor_tensor(out=rowscr[0:1, 0:TS], in0=scr_a[0:1, 0:TS],
                        in1=scr_a[0:1, 0:TS], op=ALU.mult)
        v.tensor_tensor(out=rowscr[0:1, TS:2 * TS], in0=scr_a[0:1, 2 * TS:3 * TS],
                        in1=scr_a[0:1, 2 * TS:3 * TS], op=ALU.mult)
        v.drain()
        v.tensor_tensor(out=scr_a[0:1, TS:2 * TS], in0=scr_a[0:1, TS:2 * TS],
                        in1=rowscr[0:1, 0:TS], op=ALU.subtract)
        v.tensor_tensor(out=scr_a[0:1, 3 * TS:4 * TS], in0=scr_a[0:1, 3 * TS:4 * TS],
                        in1=rowscr[0:1, TS:2 * TS], op=ALU.subtract)
        v.drain()
        v.tensor_scalar(out=scr_a[0:1, TS:2 * TS], in0=scr_a[0:1, TS:2 * TS],
                        scalar1=1e-5, scalar2=None, op0=ALU.add)
        v.tensor_scalar(out=scr_a[0:1, 3 * TS:4 * TS], in0=scr_a[0:1, 3 * TS:4 * TS],
                        scalar1=1e-5, scalar2=None, op0=ALU.add)

    @chain("scalar", "hln_r1", serial=False)
    def _(s):
        s.activation(out=rowscr[0:1, 0:TS], in_=scr_a[0:1, TS:2 * TS], func=AF.Ln)
        s.activation(out=rowscr[0:1, TS:2 * TS], in_=scr_a[0:1, 3 * TS:4 * TS], func=AF.Ln)
        s.drain()
        s.activation(out=scr_a[0:1, TS:2 * TS], in_=rowscr[0:1, 0:TS], func=AF.Exp, scale=-0.5)
        s.activation(out=scr_a[0:1, 3 * TS:4 * TS], in_=rowscr[0:1, TS:2 * TS], func=AF.Exp, scale=-0.5)

    @chain("tensor", "hmap_r1")
    def _(t):
        t.matmul(hbanks[2][0:32, :], onesr_f[0:1, 0:32], scr_a[0:1, 0:TS], start=True, stop=True)
        t.matmul(ob[2][0:32, :], onesr_f[0:1, 0:32], scr_a[0:1, TS:2 * TS], start=True, stop=True)
        t.matmul(hbanks[3][0:32, :], onesr_f[0:1, 0:32], scr_a[0:1, 2 * TS:3 * TS], start=True, stop=True)
        t.matmul(ob[3][0:32, :], onesr_f[0:1, 0:32], scr_a[0:1, 3 * TS:4 * TS], start=True, stop=True)

    @chain("vector", "hlnv_r1", serial=False)
    def _(v):
        for tt in (2, 3):
            v.tensor_tensor(out=scr_a[0:32, tt * TS:(tt + 1) * TS],
                            in0=gh[0:32, tt * TS:(tt + 1) * TS],
                            in1=hbanks[tt][0:32, :], op=ALU.subtract)
        v.drain()
        for tt in (2, 3):
            v.tensor_tensor(out=scr_a[0:32, tt * TS:(tt + 1) * TS],
                            in0=scr_a[0:32, tt * TS:(tt + 1) * TS],
                            in1=ob[tt][0:32, :], op=ALU.mult)

    @chain("scalar", "hrelu_r1")
    def _(s):
        s.activation(out=gh[0:32, 2 * TS:(3 + 1) * TS], in_=scr_a[0:32, 2 * TS:(3 + 1) * TS],
                     func=AF.Relu, bias=lnbs[:, 0:1], scale=lngs[:, 0:1])

    @chain("tensor", "gmm_all")
    def _(t):
        for tt in range(NT):
            t.matmul(ob[tt][0:8, :], g2b[:, :], gh[0:32, tt * TS:(tt + 1) * TS],
                     start=True, stop=True)

    @chain("vector", "gev_all", serial=False)
    def _(v):
        for tt in range(NT):
            v.tensor_scalar(out=gatesf[:, tt * TS:(tt + 1) * TS], in0=ob[tt][0:8, :],
                            scalar1=gb2s[:, 0:1], scalar2=None, op0=ALU.add)

    @chain("tensor", "gt_mm")
    def _(t):
        for gix in range(16):
            t.matmul(miscp[0:P, gix * 8:(gix + 1) * 8],
                     gatesf[:, gix * P:(gix + 1) * P], i8f[:, :],
                     start=(gix == 0), stop=(gix == 15))

    @chain("vector", "gt_cp")
    def _(v):
        v.tensor_copy(out=gt[:, :], in_=miscp[0:P, 0:128])

    @chain("vector", "top2a", serial=False)
    def _(v):
        G = lambda gix: gt[:, gix * 8:(gix + 1) * 8]
        W = lambda gix: wtok[:, gix * 8:(gix + 1) * 8]
        O1 = lambda gix: oh1[:, gix * 8:(gix + 1) * 8]
        O2 = lambda gix: oh2[:, gix * 8:(gix + 1) * 8]
        for g_ in range(16):
            v.tensor_reduce(out=d16[:, g_:g_ + 1], in_=G(g_), axis=mybir.AxisListType.X, op=ALU.max)
        for g_ in range(16):
            v.tensor_scalar(out=O1(g_), in0=G(g_), scalar1=d16[:, g_:g_ + 1], scalar2=None, op0=ALU.is_equal)
        for g_ in range(16):
            v.tensor_tensor(out=W(g_), in0=O1(g_), in1=c8mi[:, :], op=ALU.mult)
        for g_ in range(16):
            v.tensor_scalar(out=W(g_), in0=W(g_), scalar1=-1.0, scalar2=8.0, op0=ALU.mult, op1=ALU.add)
        for g_ in range(16):
            v.tensor_reduce(out=e16[:, g_:g_ + 1], in_=W(g_), axis=mybir.AxisListType.X, op=ALU.min)
        for g_ in range(16):
            v.tensor_scalar(out=O1(g_), in0=iot8[:, :], scalar1=e16[:, g_:g_ + 1], scalar2=None, op0=ALU.is_equal)
        for g_ in range(16):
            v.scalar_tensor_tensor(out=W(g_), in0=O1(g_), scalar=-1e30, in1=G(g_), op0=ALU.mult, op1=ALU.add)
        for g_ in range(16):
            v.tensor_reduce(out=tg1c[:, g_:g_ + 1], in_=W(g_), axis=mybir.AxisListType.X, op=ALU.max)
        for g_ in range(16):
            v.tensor_scalar(out=O2(g_), in0=W(g_), scalar1=tg1c[:, g_:g_ + 1], scalar2=None, op0=ALU.is_equal)
        for g_ in range(16):
            v.tensor_tensor(out=W(g_), in0=O2(g_), in1=c8mi[:, :], op=ALU.mult)
        for g_ in range(16):
            v.tensor_scalar(out=W(g_), in0=W(g_), scalar1=-1.0, scalar2=8.0, op0=ALU.mult, op1=ALU.add)
        for g_ in range(16):
            v.tensor_reduce(out=tg2c[:, g_:g_ + 1], in_=W(g_), axis=mybir.AxisListType.X, op=ALU.min)
        for g_ in range(16):
            v.tensor_scalar(out=O2(g_), in0=iot8[:, :], scalar1=tg2c[:, g_:g_ + 1], scalar2=None, op0=ALU.is_equal)
        for g_ in range(16):
            v.tensor_tensor(out=d16[:, g_:g_ + 1], in0=tg1c[:, g_:g_ + 1], in1=d16[:, g_:g_ + 1], op=ALU.subtract)

    @chain("tensor", "cnt_mm")
    def _(t):
        for gix in range(16):
            t.matmul(miscp[0:8, 0:1], oh1[:, gix * 8:(gix + 1) * 8], ones_bf[:, 0:1],
                     start=(gix == 0), stop=(gix == 15))
        for gix in range(16):
            t.matmul(miscp[0:8, 2:3], oh2[:, gix * 8:(gix + 1) * 8], ones_bf[:, 0:1],
                     start=(gix == 0), stop=(gix == 15))

    @chain("vector", "cnt_cp", serial=False)
    def _(v):
        v.tensor_copy(out=cnt_sb[0:8, 0:1], in_=miscp[0:8, 0:1])
        v.tensor_copy(out=cnt_sb[0:8, 1:2], in_=miscp[0:8, 2:3])

    @chain("tensor", "cnt_t")
    def _(t):
        t.matmul(sparep[0:1, 0:8], cnt_sb[0:8, 0:1], i8f[:, :], start=True, stop=True)
        t.matmul(sparep[0:1, 8:16], cnt_sb[0:8, 1:2], i8f[:, :], start=True, stop=True)

    @free("tensor", name="f0h")
    def _(t):
        t.wait_ge(d_tk, 208)
        for tt in range(NT):
            t.matmul(ob[tt][:, :], Tkb[:, :], xn_ap(0, tt * TS, TS),
                     start=True, stop=False)
        for tt in range(2):
            t.matmul(sbk[tt][:, :], Pbig[:, :], xn_ap(0, tt * TS, TS),
                     start=True, stop=True).then_inc(s_scanm, 1)
        for si in range(NS):
            if si == 0:
                t.wait_ge(d_w, 16)
            if si == 16:
                t.wait_ge(d_w, 32)
            half = si // 16
            wsl = whbuf[:, half * 16 * P + (si % 16) * P: half * 16 * P + (si % 16) * P + P]
            for tt in range(NT):
                mm = t.matmul(ob[tt][:, :], wsl, xn_ap(si, tt * TS, TS),
                              start=False, stop=False)
                if tt == NT - 1 and (si == 15 or si == 31):
                    mm.then_inc(s_wfree, 1)

    @chain("vector", "cc_cp")
    def _(v):
        v.tensor_copy(out=ccs[0:1, :], in_=sparep[0:1, 0:16])

    @chain("gpsimd", "cc_go")
    def _(g):
        g.dma_start(out=d_ccin.ap()[:, :], in_=ccs[0:1, :]).then_inc(d_cc, 16)
        g.wait_ge(d_cc, 16)
        g.collective_compute(
            "AllReduce", ALU.add,
            replica_groups=[list(range(NCORES))],
            ins=[d_ccin.ap()],
            outs=[d_ccout.ap()],
        ).then_inc(c_cc, 1)

    @chain("scalar", "top2exp")
    def _(s):
        s.activation(out=e16[:, :], in_=d16[:, :], func=AF.Exp)

    @chain("vector", "top2b")
    def _(v):
        v.tensor_scalar(out=tg1c[:, :], in0=e16[:, :], scalar1=1.0, scalar2=None, op0=ALU.add)
        v.reciprocal(out=tg1c[:, :], in_=tg1c[:, :])
        v.tensor_tensor(out=tg2c[:, :], in0=e16[:, :], in1=tg1c[:, :], op=ALU.mult)

    @chain("gpsimd", "cc_fin")
    def _(g):
        g.wait_ge(c_cc, 1)
        g.dma_start(out=cct[0:1, :], in_=d_ccout.ap()[:, :]).then_inc(d_cc, 16)
        g.wait_ge(d_cc, 32)

    @chain("vector", "alr_cmp")
    def _(v):
        v.tensor_scalar(out=alr[0:1, :], in0=cct[0:1, :], scalar1=CAP, scalar2=None,
                        op0=ALU.is_le)

    @chain("tensor", "al_bcast")
    def _(t):
        t.matmul(miscp[:, 0:8], onesr_f[0:1, 0:P], alr[0:1, 0:8], start=True, stop=True)
        t.matmul(miscp[:, 8:16], onesr_f[0:1, 0:P], alr[0:1, 8:16], start=True, stop=True)

    @chain("vector", "wtok_fin", serial=False)
    def _(v):
        v.tensor_copy(out=alb[:, :], in_=miscp[:, 0:16])
        v.drain()
        SL = lambda gix: slice(gix * 8, (gix + 1) * 8)
        for g_ in range(16):
            v.tensor_scalar(out=oh1[:, SL(g_)], in0=oh1[:, SL(g_)],
                            scalar1=tg1c[:, g_:g_ + 1], scalar2=None, op0=ALU.mult)
        for g_ in range(16):
            v.tensor_scalar(out=oh2[:, SL(g_)], in0=oh2[:, SL(g_)],
                            scalar1=tg2c[:, g_:g_ + 1], scalar2=None, op0=ALU.mult)
        for g_ in range(16):
            v.tensor_tensor(out=oh1[:, SL(g_)], in0=oh1[:, SL(g_)], in1=alb[:, 0:8], op=ALU.mult)
        for g_ in range(16):
            v.tensor_tensor(out=oh2[:, SL(g_)], in0=oh2[:, SL(g_)], in1=alb[:, 8:16], op=ALU.mult)
        for g_ in range(16):
            v.tensor_tensor(out=wtok[:, SL(g_)], in0=oh1[:, SL(g_)], in1=oh2[:, SL(g_)], op=ALU.add)

    def wexp_tt(tt):
        @chain("tensor", f"wfm{tt}")
        def _(t):
            for gg in range(4):
                gix = tt * 4 + gg
                t.matmul(sparep[0:8, gg * P:(gg + 1) * P], wtok[:, gix * 8:(gix + 1) * 8],
                         i128b[:, :], start=True, stop=True)

        @chain("vector", f"wfmcp{tt}")
        def _(v):
            v.tensor_copy(out=gh[0:8, tt * TS:(tt + 1) * TS], in_=sparep[0:8, :])

        @chain("tensor", f"wexp{tt}")
        def _(t):
            t.matmul(miscp[:, :], esel[:, :], gh[0:8, tt * TS:(tt + 1) * TS],
                     start=True, stop=True)

        @chain("vector", f"tmpsc{tt}", )
        def _(v):
            v.tensor_tensor(out=tmp_sc[:, tt * TS:(tt + 1) * TS],
                            in0=tmp_bf[:, tt * TS:(tt + 1) * TS], in1=miscp[:, :],
                            op=ALU.mult)

    for tt in range(NT):
        wexp_tt(tt)

    @chain("vector", "wexp_done")
    def _(v):
        pass

    if DBG:
        @chain("gpsimd", "dbg_fill")
        def _(g):
            pairs = [
                (d_dbg.ap()[:, 0:2048], actbuf[:, 0:2048]),
                (d_dbg.ap()[0:1, 2048:3072], rowscr[0:1, :]),
                (d_dbg.ap()[0:8, 3072:5120], gatesf[:, :]),
                (d_dbg.ap()[0:1, 5120:5121], stp1[:, :]),
                (d_dbg.ap()[0:1, 5121:5122], dD1[:, :]),
                (d_dbg.ap()[0:1, 5122:5250], Krow[0:1, :]),
                (d_dbg.ap()[:, 5250:5378], Mhi[:, :]),
                (d_dbg.ap()[:, 5378:5506], Tkb[:, :]),
                (d_dbg.ap()[:, 5506:5634], map_mu[:, 0:128]),
                (d_dbg.ap()[:, 5634:5762], map_rs[:, 0:128]),
                (d_dbg.ap()[:, 5762:5890], tmp_bf[:, 0:128]),
                (d_dbg.ap()[:, 5890:6018], tmp_sc[:, 0:128]),
                (d_dbg.ap()[0:P, 6018:6034], alb[:, :]),
                (d_dbg.ap()[0:1, 6034:6050], cct[0:1, :]),
            ]
            n = 0
            for dst, src in pairs:
                g.dma_start(out=dst, in_=src, allow_slow_non_contiguous=True).then_inc(d_tk, 16)
                n += 1
            g.wait_ge(d_tk, 16 * (16 + n))


    banks6 = [ob[0], ob[1], ob[2], ob[3], miscp, sparep]

    @free("tensor")
    def _(t):
        for f in range(NS):
            for tt in range(NT):
                g = 4 * f + tt
                bk = banks6[g % 6]
                if f == 0:
                    if tt < 2:
                        continue
                    t.wait_ge(ch, CH_IDX["wexp_done"] + 1)
                    t.wait_ge(s_split, (tt - 2) + 1)
                    t.matmul(sbk[tt % 2][:, :], Pbig[:, :], xn_ap(f, tt * TS, TS),
                             start=True, stop=True).then_inc(s_scanm, 1)
                    continue
                if g - 6 >= 0:
                    t.wait_ge(s_evac, g - 6 + 1)
                t.wait_ge(s_split, 4 * (f - 1) + tt + 1)
                t.matmul(bk[:, :], Tkb[:, :], xn_ap(f, tt * TS, TS),
                         start=True, stop=False)
                t.matmul(bk[:, :], Qbig[:, :], S_hi[:, tt * TS:(tt + 1) * TS],
                         start=False, stop=False)
                if f <= NS - 2:
                    if tt >= 2:
                        t.wait_ge(s_split, 4 * f + (tt - 2) + 1)
                    sbank = sbk[tt % 2]
                    t.matmul(sbank[:, :], Mhi[:, :], S_hi[:, tt * TS:(tt + 1) * TS],
                             start=True, stop=False)
                    t.matmul(sbank[:, :], Mhi[:, :], S_lo[:, tt * TS:(tt + 1) * TS],
                             start=False, stop=False)
                    t.matmul(sbank[:, :], Mlo[:, :], S_hi[:, tt * TS:(tt + 1) * TS],
                             start=False, stop=False)
                    t.matmul(sbank[:, :], Pbig[:, :], xn_ap(f, tt * TS, TS),
                             start=False, stop=True).then_inc(s_scanm, 1)
            if f >= 1:
                for si in range(NS):
                    if si == 0:
                        t.wait_ge(d_w, 16 * (2 * f + 1))
                    if si == 16:
                        t.wait_ge(d_w, 16 * (2 * f + 2))
                    half = (2 * f + (si // 16)) % 2
                    wsl = whbuf[:, half * 16 * P + (si % 16) * P: half * 16 * P + (si % 16) * P + P]
                    for tt in range(NT):
                        mm = t.matmul(banks6[(4 * f + tt) % 6][:, :], wsl,
                                      xn_ap(si, tt * TS, TS), start=False, stop=False)
                        if tt == NT - 1 and (si == 15 or si == 31):
                            mm.then_inc(s_wfree, 1)
            for tt in range(NT):
                t.matmul(banks6[(4 * f + tt) % 6][:, :], lBb[:, f * P:(f + 1) * P],
                         tmp_sc[:, tt * TS:(tt + 1) * TS],
                         start=False, stop=True).then_inc(s_main, 1)

    @free("scalar")
    def _(s):
        for f in range(NS):
            if f <= NS - 2:
                for tt in range(NT):
                    s.wait_ge(s_scanm, 4 * f + tt + 1)
                    s.copy(out=S_hi[:, tt * TS:(tt + 1) * TS],
                           in_=sbk[tt % 2][:, :]).then_inc(sc_shi, 1)


    @free("vector")
    def _(v):
        for f in range(NS):
            if f <= NS - 2:
                for tt in range(NT):
                    v.wait_ge(sc_shi, 4 * f + tt + 1)
                    v.tensor_tensor(out=S_lo[:, tt * TS:(tt + 1) * TS],
                                    in0=sbk[tt % 2][:, :],
                                    in1=S_hi[:, tt * TS:(tt + 1) * TS],
                                    op=ALU.subtract).then_inc(s_split, 1)
            v.wait_ge(d_xr, 16 * (f + 1))
            if f >= 1:
                v.wait_ge(d_outS, 16 * f)
            for tt in range(NT):
                g = 4 * f + tt
                v.wait_ge(s_main, g + 1)
                v.tensor_tensor(out=scr_a[:, tt * TS:(tt + 1) * TS],
                                in0=banks6[g % 6][:, :],
                                in1=lAx2[:, tt * TS:(tt + 1) * TS],
                                op=ALU.add).then_inc(s_evac, 1)

    @free("sync")
    def _(sync):
        for f in range(NS):
            sync.wait_ge(s_evac, 4 * (f + 1))
            sync.dma_start(out=d_out.ap()[f * P:(f + 1) * P, :],
                           in_=scr_a[:, :]).then_inc(d_outS, 16)
        sync.wait_ge(d_outS, 16 * NS)

    @free("gpsimd")
    def _(g):
        wr = d_wT.ap()
        xr = d_xT.ap().rearrange("(s p) t -> s p t", p=P)
        g.dma_start(out=lAx2[:, 0:T], in_=xr[0]).then_inc(d_xr, 16)
        for H in range(2, 2 * NS):
            f, h = H // 2, H % 2
            g.wait_ge(s_wfree, H - 1)
            g.dma_start(
                out=whbuf[:, (H % 2) * 16 * P:(H % 2 + 1) * 16 * P],
                in_=wr[f, :, 16 * h:16 * (h + 1), :].rearrange("p s o -> p (s o)"),
            ).then_inc(d_w, 16)
            if H % 2 == 1:
                f2 = H // 2
                g.wait_ge(s_evac, 4 * f2)
                g.dma_start(out=lAx2[:, 0:T], in_=xr[f2]).then_inc(d_xr, 16)

    _SERIAL_OPS = {"tensor_scalar", "tensor_tensor", "tensor_copy", "reciprocal",
                   "memset", "scalar_tensor_tensor", "tensor_reduce", "activation",
                   "copy", "iota", "tensor_single_scalar"}

    class _SerialEng:
        def __init__(self, eng):
            self._e = eng

        def __getattr__(self, name):
            attr = getattr(self._e, name)
            if callable(attr) and name in _SERIAL_OPS:
                def wrapped(*a, **k):
                    r = attr(*a, **k)
                    self._e.drain()
                    return r
                return wrapped
            return attr

    with nc.Block() as block:
        def _find(name):
            for i2, e in enumerate(plan):
                if e[4] == name:
                    return i2
            raise KeyError(name)

        def _extract(names):
            return [plan.pop(_find(nm)) for nm in names]

        stA = _extract([f"stu{u}" for u in range(NS)])
        stB = _extract([f"stu{u}" for u in range(NS, 2 * NS)])
        rowsA = _extract(["rowsA", "lnA", "mursA"])
        rowsB = _extract(["rowsB", "lnB", "mursB"])
        sqA = _extract(["sq0"])
        sqB = _extract(["sq1"])
        pos = _find("atil_mm") + 1
        for e in stA:
            plan.insert(pos, e)
            pos += 2
        for j, e in enumerate(rowsA):
            plan.insert(pos + j, e)
        pos += len(rowsA)
        plan.insert(pos, sqB[0]); pos += 1
        for e in stB:
            if pos < len(plan):
                plan.insert(pos, e); pos += 2
            else:
                plan.append(e); pos = len(plan)
        ise = _find("setup_end")
        for j, e in enumerate(rowsB):
            plan.insert(ise + j, e)
        plan.insert(_find("stp") + 1, sqA[0])
        finalize_plan()

        def emit(engine_name, eng):
            ci = 0
            for (e, kind, fn, serial, _nm) in plan:
                if kind == "chain":
                    if e == engine_name:
                        eng.wait_ge(ch, ci)
                        use_ser = serial and engine_name in ("vector", "scalar", "gpsimd")
                        fn(_SerialEng(eng) if use_ser else eng)
                        eng.drain().then_inc(ch, 1)
                    ci += 1
                else:
                    if e == engine_name:
                        fn(eng)

        @block.sync
        def _(sync):
            emit("sync", sync)

        @block.gpsimd
        def _(g):
            emit("gpsimd", g)

        @block.tensor
        def _(t):
            emit("tensor", t)

        @block.vector
        def _(v):
            emit("vector", v)

        @block.scalar
        def _(s):
            emit("scalar", s)

    return nc



def kernel(**inputs):
    x = np.ascontiguousarray(np.asarray(inputs["L0_attn_u"], np.float32))
    weight = np.asarray(inputs["L0_ffn_weight"], np.float32)
    lora_A = np.asarray(inputs["L0_ffn_lora_A"], np.float32)
    lora_B = np.asarray(inputs["L0_ffn_lora_B"], np.float32)
    gw1 = np.asarray(inputs["L0_ffn_gw1"], np.float32)

    xf = x.reshape(-1, DM)
    shared = {
        "wT": np.ascontiguousarray(weight.reshape(NS, P, NS, P).transpose(0, 3, 2, 1)),
        "lA": np.ascontiguousarray(lora_A.transpose(1, 0, 2).reshape(NS, P, P)),
        "lB": np.ascontiguousarray(lora_B.reshape(P, NS, P).transpose(1, 0, 2)),
        "g1": np.ascontiguousarray(gw1.T.reshape(NS, P, 32)),
        "g2T": np.ascontiguousarray(np.asarray(inputs["L0_ffn_gw2"], np.float32).T),
        "gb1": np.ascontiguousarray(np.asarray(inputs["L0_ffn_gb1"], np.float32).reshape(32, 1)),
        "lng": np.ascontiguousarray(np.asarray(inputs["L0_ffn_ln_gamma"], np.float32).reshape(32, 1)),
        "lnb": np.ascontiguousarray(np.asarray(inputs["L0_ffn_ln_beta"], np.float32).reshape(32, 1)),
        "gb2": np.ascontiguousarray(np.asarray(inputs["L0_ffn_gb2"], np.float32).reshape(8, 1)),
        "ssm": np.ascontiguousarray(np.concatenate([
            np.asarray(inputs["L0_attn_Lambda_ri"], np.float32),
            np.asarray(inputs["L0_attn_p_ri"], np.float32),
            np.asarray(inputs["L0_attn_q_ri"], np.float32),
            np.asarray(inputs["L0_attn_Ct_ri"], np.float32).reshape(64, 2),
            np.asarray(inputs["L0_attn_Bp"], np.float32).reshape(64, 1)], axis=1)),
        "vcr": np.ascontiguousarray(np.asarray(inputs["L0_attn_Vc_ri"], np.float32).transpose(1, 0, 2)[:, :, 0]),
        "vci": np.ascontiguousarray(np.asarray(inputs["L0_attn_Vc_ri"], np.float32).transpose(1, 0, 2)[:, :, 1]),
        "dsc": np.ascontiguousarray(np.concatenate([
            np.asarray(inputs["L0_attn_D"], np.float32).reshape(1),
            np.asarray(inputs["L0_attn_log_step"], np.float32).reshape(1)]).reshape(1, 2)),
    }

    nc = _build()
    in_maps = []
    for c in range(NCORES):
        m = {"xT": np.ascontiguousarray(xf[c * T:(c + 1) * T].T)}
        m.update(shared)
        in_maps.append(m)

    res = run_bass_kernel_spmd(nc, in_maps, core_ids=list(range(NCORES)))
    out = np.empty((NCORES * T, DM), np.float32)
    for c in range(NCORES):
        out[c * T:(c + 1) * T] = res.results[c]["outT"].T
    return out.reshape(8, T, DM)




# version 20
# speedup vs baseline: 1.0543x; 1.0543x over previous
import numpy as np
import concourse.bass as bass
import concourse.mybir as mybir
from concourse.bass_utils import run_bass_kernel_spmd

F32 = mybir.dt.float32
BF16 = mybir.dt.bfloat16
AF = mybir.ActivationFunctionType
ALU = mybir.AluOpType

NCORES = 8
DBG = False
P = 128
NS = 32
T = 2048
TS = 512
NT = 4
DM = 4096
CAP = 6144.0
ALPHA = 1.0 / 16.0
N_SMALL = 13


def _build():
    nc = bass.Bass("TRN2", target_bir_lowering=False, detect_race_conditions=False)

    d_xT = nc.dram_tensor("xT", [DM, T], F32, kind="ExternalInput")
    d_wT = nc.dram_tensor("wT", [NS, P, NS, P], F32, kind="ExternalInput")
    d_lA = nc.dram_tensor("lA", [NS, P, P], F32, kind="ExternalInput")
    d_lB = nc.dram_tensor("lB", [NS, P, P], F32, kind="ExternalInput")
    d_g1 = nc.dram_tensor("g1", [NS, P, 32], F32, kind="ExternalInput")
    d_g2T = nc.dram_tensor("g2T", [32, 8], F32, kind="ExternalInput")
    d_gb1 = nc.dram_tensor("gb1", [32, 1], F32, kind="ExternalInput")
    d_lng = nc.dram_tensor("lng", [32, 1], F32, kind="ExternalInput")
    d_lnb = nc.dram_tensor("lnb", [32, 1], F32, kind="ExternalInput")
    d_gb2 = nc.dram_tensor("gb2", [8, 1], F32, kind="ExternalInput")
    d_ssm = nc.dram_tensor("ssm", [64, 9], F32, kind="ExternalInput")
    d_vcr = nc.dram_tensor("vcr", [64, 64], F32, kind="ExternalInput")
    d_vci = nc.dram_tensor("vci", [64, 64], F32, kind="ExternalInput")
    d_dsc = nc.dram_tensor("dsc", [1, 2], F32, kind="ExternalInput")
    d_out = nc.dram_tensor("outT", [DM, T], F32, kind="ExternalOutput")
    if DBG:
        d_dbg = nc.dram_tensor("dbg", [P, 8192], F32, kind="ExternalOutput")
    d_ccin = nc.dram_tensor("ccin", [1, 16], F32)
    d_ccout = nc.dram_tensor("ccout", [1, 16], F32)

    import contextlib
    es = contextlib.ExitStack()
    sb = lambda name, shape, dt: es.enter_context(nc.sbuf_tensor(name, shape, dt))

    actbuf = sb("actbuf", [P, NS * T], BF16)
    whbuf = sb("whbuf", [P, 2 * 16 * P], BF16)
    lBb = sb("lBb", [P, NS * P], BF16)
    lAx = sb("lAx", [P, NS * P], BF16)
    g1b = sb("g1b", [P, NS * 32], BF16)
    scr_a = sb("scr_a", [P, T], F32)
    tmp_bf = sb("tmp_bf", [P, T], BF16)
    tmp_sc = sb("tmp_sc", [P, T], BF16)
    gh = sb("gh", [32, T], BF16)
    lAx2 = sb("lAx2", [P, T], BF16)
    rowscr = sb("rowscr", [1, 1024], BF16)
    i128f = sb("i128f", [P, P], F32)
    i128b = sb("i128b", [P, P], BF16)
    i8f = sb("i8f", [8, 8], F32)
    i8b = sb("i8b", [8, 8], BF16)
    iot_f = sb("iot_f", [P, P], F32)
    iot_p = sb("iot_p", [P, 1], F32)
    iot8 = sb("iot8", [P, 8], F32)
    c8mi = sb("c8mi", [P, 8], F32)
    esel = sb("esel", [8, P], BF16)
    ones_bf = sb("ones_bf", [P, 1], BF16)
    ones_f = sb("ones_f", [P, 1], F32)
    onesr_bf = sb("onesr_bf", [1, P], BF16)
    onesr_f = sb("onesr_f", [1, P], F32)
    Tkb = sb("Tkb", [P, P], BF16)
    Pbig = sb("Pbig", [P, P], BF16)
    Qbig = sb("Qbig", [P, P], BF16)
    Mhi = sb("Mhi", [P, P], BF16)
    Mlo = sb("Mlo", [P, P], BF16)
    gt = sb("gt", [P, P], F32)
    oh1 = sb("oh1", [P, P], BF16)
    oh2 = sb("oh2", [P, P], BF16)
    wtok = sb("wtok", [P, P], BF16)
    d16 = sb("d16", [P, 16], F32)
    e16 = sb("e16", [P, 16], F32)
    tg1c = sb("tg1c", [P, 16], F32)
    tg2c = sb("tg2c", [P, 16], F32)
    alb = sb("alb", [P, 16], F32)
    cnt_sb = sb("cnt_sb", [8, 2], F32)
    ccs = sb("ccs", [1, 16], F32)
    cct = sb("cct", [1, 16], F32)
    alr = sb("alr", [1, 16], F32)
    Krow = sb("Krow", [1, P], F32)
    kcol = sb("kcol", [P, 1], F32)
    stp1 = sb("stp1", [1, 1], F32)
    dD1 = sb("dD1", [1, 1], F32)
    gb1s = sb("gb1s", [32, 1], F32)
    lngs = sb("lngs", [32, 1], F32)
    lnbs = sb("lnbs", [32, 1], F32)
    gb2s = sb("gb2s", [8, 1], F32)
    g2f = sb("g2f", [32, 8], F32)
    g2b = sb("g2b", [32, 8], BF16)

    es2 = contextlib.ExitStack()
    sb2 = lambda name, shape, dt: es2.enter_context(nc.sbuf_tensor(name, shape, dt))
    MR = [sb2(f"MR{k}", [64, 64], F32) for k in range(10)]
    MI = [sb2(f"MI{k}", [64, 64], F32) for k in range(10)]
    MTR = [sb2(f"MTR{k}", [64, 64], F32) for k in range(10)]
    MTI = [sb2(f"MTI{k}", [64, 64], F32) for k in range(10)]
    negscr = sb2("negscr", [64, 64], F32)
    negscr2 = sb2("negscr2", [64, 64], F32)
    vec = sb2("vecs", [64, 40], F32)
    Pr = sb2("Pr", [64, P], F32)
    Pi = sb2("Pi", [64, P], F32)
    Wr = sb2("Wr", [64, P], F32)
    Wi = sb2("Wi", [64, P], F32)
    Tk32 = sb2("Tk32", [P, P], F32)
    i64f = sb2("i64f", [64, 64], F32)
    arow = sb2("arow", [1, 4 * 64], F32)
    mrow = sb2("mrow", [1, 2 * 64], F32)
    sc11 = sb2("sc11", [1, 8], F32)
    es2.close()

    S_hi = sb("S_hi", [P, T], BF16)
    S_lo = sb("S_lo", [P, T], BF16)
    es3 = contextlib.ExitStack()
    map_mu = es3.enter_context(nc.sbuf_tensor("map_mu", [P, T], BF16))
    map_rs = es3.enter_context(nc.sbuf_tensor("map_rs", [P, T], BF16))
    es3.close()
    gatesf = sb("gatesf", [8, T], F32)

    def V(i):
        return vec[:, i:i + 1]

    (LAM_R, LAM_I, P_R, P_I, Q_R, Q_I, CT_R, CT_I, BP_, STP64, HSTP,
     DM_R, DM_I, DP_R, DP_I, DMI_R, DMI_I, U_R, U_I, A_R, A_I,
     B_R, B_I, BT_R, BT_I, G_R, G_I, M_R, M_I, T0, T1, T2, T3,
     V_R, V_I, W_R, W_I, R_R, R_I) = range(39)

    esp = contextlib.ExitStack()
    pb = lambda name: esp.enter_context(nc.psum_tensor(name, [P, TS], F32))
    ob = [pb(f"ob{t}") for t in range(NT)]
    sbk = [pb("sbk0"), pb("sbk1")]
    miscp = pb("miscp")
    sparep = pb("sparep")
    hbanks = [sbk[0], sbk[1], miscp, sparep]

    ess = contextlib.ExitStack()
    sem = lambda name: ess.enter_context(nc.semaphore(name))
    ch = sem("ch")
    d_small = sem("d_small")
    d_gsm = sem("d_gsm")
    d_xbf = sem("d_xbf")
    d_w = sem("d_w")
    d_xr = sem("d_xr")
    d_outS = sem("d_outS")
    d_cc = sem("d_cc")
    d_tk = sem("d_tk")
    c_cc = sem("c_cc")
    g_xn = sem("g_xn")
    t_st = sem("t_st")
    v_sq = sem("v_sq")
    v_xn = sem("v_xn")
    s_wfree = sem("s_wfree")
    s_main = sem("s_main")
    s_evac = sem("s_evac")
    s_scanm = sem("s_scanm")
    sc_shi = sem("sc_shi")
    s_split = sem("s_split")

    plan = []
    CHN = [0]
    CH_IDX = {}

    def chain(eng, name=None, serial=True):
        def deco(fn):
            plan.append([eng, "chain", fn, serial, name])
            CHN[0] += 1
        return deco

    def free(eng, name=None):
        def deco(fn):
            plan.append([eng, "free", fn, False, name])
        return deco

    def finalize_plan():
        CH_IDX.clear()
        ci = 0
        for e in plan:
            if e[1] == "chain":
                if e[4]:
                    CH_IDX[e[4]] = ci
                ci += 1

    def xn_ap(s, t0, n):
        return actbuf[:, s * T + t0: s * T + t0 + n]

    def cmul_vec(v_, or_, oi_, ar, ai, br, bi):
        v_.tensor_tensor(out=V(T0), in0=V(ar), in1=V(br), op=ALU.mult)
        v_.tensor_tensor(out=V(T1), in0=V(ai), in1=V(bi), op=ALU.mult)
        v_.tensor_tensor(out=V(or_), in0=V(T0), in1=V(T1), op=ALU.subtract)
        v_.tensor_tensor(out=V(T0), in0=V(ar), in1=V(bi), op=ALU.mult)
        v_.tensor_tensor(out=V(T1), in0=V(ai), in1=V(br), op=ALU.mult)
        v_.tensor_tensor(out=V(oi_), in0=V(T0), in1=V(T1), op=ALU.add)

    @free("sync")
    def _(sync):
        aps = [
            (vec[:, LAM_R:LAM_R + 9], d_ssm.ap()[:, :]),
            (MR[8][:, :], d_vcr.ap()[:, :]),
            (MI[8][:, :], d_vci.ap()[:, :]),
            (dD1[:, :], d_dsc.ap()[:, 0:1]),
            (stp1[:, :], d_dsc.ap()[:, 1:2]),
            (gb1s[:, :], d_gb1.ap()[:, :]),
            (lngs[:, :], d_lng.ap()[:, :]),
            (lnbs[:, :], d_lnb.ap()[:, :]),
            (gb2s[:, :], d_gb2.ap()[:, :]),
            (g2f[:, :], d_g2T.ap()[:, :]),
        ]
        assert len(aps) == 10
        for dst, src in aps:
            sync.dma_start(out=dst, in_=src, allow_slow_non_contiguous=True).then_inc(d_small, 16)

    NSMALL = 10

    @chain("gpsimd", "iotas")
    def _(g):
        g.iota(iot_f[:, :], pattern=[[1, P]], base=0, channel_multiplier=0,
               allow_small_or_imprecise_dtypes=True)
        g.iota(iot_p[:, :], pattern=[[0, 1]], base=0, channel_multiplier=1,
               allow_small_or_imprecise_dtypes=True)
        g.iota(iot8[:, :], pattern=[[1, 8]], base=0, channel_multiplier=0,
               allow_small_or_imprecise_dtypes=True)

    @chain("vector", "consts")
    def _(v):
        v.tensor_scalar(out=i128f[:, :], in0=iot_f[:, :], scalar1=iot_p[:, 0:1],
                        scalar2=None, op0=ALU.is_equal)
        v.tensor_copy(out=i128b[:, :], in_=i128f[:, :])
        v.tensor_scalar(out=i8f[:, :], in0=iot_f[0:8, 0:8], scalar1=iot_p[0:8, 0:1],
                        scalar2=None, op0=ALU.is_equal)
        v.tensor_copy(out=i8b[:, :], in_=i8f[:, :])
        v.tensor_scalar(out=i64f[:, :], in0=iot_f[0:64, 0:64], scalar1=iot_p[0:64, 0:1],
                        scalar2=None, op0=ALU.is_equal)
        v.tensor_scalar(out=c8mi[:, :], in0=iot8[:, :], scalar1=-1.0, scalar2=8.0,
                        op0=ALU.mult, op1=ALU.add)
        v.memset(ones_bf[:, :], 1.0)
        v.memset(ones_f[:, :], 1.0)
        v.memset(onesr_bf[:, :], 1.0)
        v.memset(onesr_f[:, :], 1.0)
        v.wait_ge(d_small, 16 * NSMALL)
        v.tensor_copy(out=g2b[:, :], in_=g2f[:, :])

    @chain("gpsimd", "esel_iota")
    def _(g):
        g.iota(Tk32[0:8, 0:P], pattern=[[1, 8], [0, 16]], base=0,
               channel_multiplier=0, allow_small_or_imprecise_dtypes=True)

    @chain("vector", "esel")
    def _(v):
        v.tensor_scalar(out=esel[:, :], in0=Tk32[0:8, 0:P], scalar1=iot_p[0:8, 0:1],
                        scalar2=ALPHA, op0=ALU.is_equal, op1=ALU.mult)

    @free("gpsimd")
    def _(g):
        xr = d_xT.ap().rearrange("(s p) t -> s p t", p=P)
        for s in range(NS):
            g.dma_start(out=actbuf[:, s * T:(s + 1) * T], in_=xr[s]).then_inc(d_xbf, 16)
        g.dma_start(out=lAx[:, :].rearrange("p (s e) -> p s e", s=NS), in_=d_lA.ap().rearrange("s p e -> p s e")).then_inc(d_gsm, 16)
        g.dma_start(out=lBb[:, :].rearrange("e (f o) -> e f o", f=NS), in_=d_lB.ap().rearrange("f e o -> e f o")).then_inc(d_gsm, 16)
        g.dma_start(out=g1b[:, :].rearrange("p (s j) -> p s j", s=NS), in_=d_g1.ap().rearrange("s p j -> p s j")).then_inc(d_gsm, 16)
        wr = d_wT.ap()
        for H in range(2):
            f, h = H // 2, H % 2
            g.dma_start(
                out=whbuf[:, (H % 2) * 16 * P:(H % 2 + 1) * 16 * P],
                in_=wr[f, :, 16 * h:16 * (h + 1), :].rearrange("p s o -> p (s o)"),
            ).then_inc(d_w, 16)

    @chain("scalar", "stp")
    def _(s):
        s.wait_ge(d_small, 16 * NSMALL)
        s.activation(out=stp1[:, :], in_=stp1[:, :], func=AF.Exp)

    @chain("tensor", "stp_bcast")
    def _(t):
        t.matmul(miscp[0:64, 0:1], onesr_f[0:1, 0:64], stp1[0:1, 0:1], start=True, stop=True)

    @chain("vector", "stp_vec")
    def _(v):
        v.tensor_copy(out=V(STP64), in_=miscp[0:64, 0:1])
        v.tensor_scalar(out=V(HSTP), in0=V(STP64), scalar1=0.5, scalar2=None, op0=ALU.mult)
        v.tensor_tensor(out=V(T0), in0=V(HSTP), in1=V(LAM_R), op=ALU.mult)
        v.tensor_tensor(out=V(T1), in0=V(HSTP), in1=V(LAM_I), op=ALU.mult)
        v.tensor_scalar(out=V(DM_R), in0=V(T0), scalar1=-1.0, scalar2=1.0, op0=ALU.mult, op1=ALU.add)
        v.tensor_scalar(out=V(DM_I), in0=V(T1), scalar1=-1.0, scalar2=None, op0=ALU.mult)
        v.tensor_scalar(out=V(DP_R), in0=V(T0), scalar1=1.0, scalar2=None, op0=ALU.add)
        v.tensor_copy(out=V(DP_I), in_=V(T1))
        v.tensor_tensor(out=V(T0), in0=V(DM_R), in1=V(DM_R), op=ALU.mult)
        v.tensor_tensor(out=V(T1), in0=V(DM_I), in1=V(DM_I), op=ALU.mult)
        v.tensor_tensor(out=V(T2), in0=V(T0), in1=V(T1), op=ALU.add)
        v.reciprocal(out=V(T3), in_=V(T2))
        v.tensor_tensor(out=V(DMI_R), in0=V(DM_R), in1=V(T3), op=ALU.mult)
        v.tensor_tensor(out=V(T0), in0=V(DM_I), in1=V(T3), op=ALU.mult)
        v.tensor_scalar(out=V(DMI_I), in0=V(T0), scalar1=-1.0, scalar2=None, op0=ALU.mult)
        v.tensor_tensor(out=V(U_R), in0=V(HSTP), in1=V(P_R), op=ALU.mult)
        v.tensor_tensor(out=V(U_I), in0=V(HSTP), in1=V(P_I), op=ALU.mult)
        cmul_vec(v, A_R, A_I, DMI_R, DMI_I, U_R, U_I)
        v.tensor_tensor(out=V(T0), in0=V(Q_R), in1=V(DMI_R), op=ALU.mult)
        v.tensor_tensor(out=V(T1), in0=V(Q_I), in1=V(DMI_I), op=ALU.mult)
        v.tensor_tensor(out=V(R_R), in0=V(T0), in1=V(T1), op=ALU.add)
        v.tensor_tensor(out=V(T0), in0=V(Q_R), in1=V(DMI_I), op=ALU.mult)
        v.tensor_tensor(out=V(T1), in0=V(Q_I), in1=V(DMI_R), op=ALU.mult)
        v.tensor_tensor(out=V(R_I), in0=V(T0), in1=V(T1), op=ALU.subtract)
        v.tensor_tensor(out=V(T0), in0=V(Q_R), in1=V(A_R), op=ALU.mult)
        v.tensor_tensor(out=V(T1), in0=V(Q_I), in1=V(A_I), op=ALU.mult)
        v.tensor_tensor(out=V(T2), in0=V(T0), in1=V(T1), op=ALU.add)
        v.tensor_tensor(out=V(T0), in0=V(Q_R), in1=V(A_I), op=ALU.mult)
        v.tensor_tensor(out=V(T1), in0=V(Q_I), in1=V(A_R), op=ALU.mult)
        v.tensor_tensor(out=V(T3), in0=V(T0), in1=V(T1), op=ALU.subtract)

    @chain("tensor", "b_sden_mm")
    def _(t):
        t.wait_ge(d_small, 16 * NSMALL)
        t.matmul(miscp[0:64, 1:2], MR[8][:, :], vec[:, BP_:BP_ + 1], start=True, stop=True)
        t.matmul(miscp[0:64, 2:3], MI[8][:, :], vec[:, BP_:BP_ + 1], start=True, stop=True)
        t.matmul(miscp[0:1, 3:4], ones_f[0:64, 0:1], vec[:, T2:T2 + 1], start=True, stop=True)
        t.matmul(miscp[0:1, 4:5], ones_f[0:64, 0:1], vec[:, T3:T3 + 1], start=True, stop=True)

    @chain("vector", "sden")
    def _(v):
        v.tensor_copy(out=V(B_R), in_=miscp[0:64, 1:2])
        v.tensor_copy(out=V(B_I), in_=miscp[0:64, 2:3])
        v.tensor_scalar(out=sc11[0:1, 0:1], in0=miscp[0:1, 3:4], scalar1=1.0,
                        scalar2=None, op0=ALU.add)
        v.tensor_copy(out=sc11[0:1, 1:2], in_=miscp[0:1, 4:5])
        v.tensor_tensor(out=sc11[0:1, 2:3], in0=sc11[0:1, 0:1], in1=sc11[0:1, 0:1], op=ALU.mult)
        v.tensor_tensor(out=sc11[0:1, 3:4], in0=sc11[0:1, 1:2], in1=sc11[0:1, 1:2], op=ALU.mult)
        v.tensor_tensor(out=sc11[0:1, 4:5], in0=sc11[0:1, 2:3], in1=sc11[0:1, 3:4], op=ALU.add)
        v.reciprocal(out=sc11[0:1, 5:6], in_=sc11[0:1, 4:5])
        v.tensor_tensor(out=sc11[0:1, 6:7], in0=sc11[0:1, 0:1], in1=sc11[0:1, 5:6], op=ALU.mult)
        v.tensor_tensor(out=sc11[0:1, 7:8], in0=sc11[0:1, 1:2], in1=sc11[0:1, 5:6], op=ALU.mult)
        v.tensor_scalar(out=sc11[0:1, 7:8], in0=sc11[0:1, 7:8], scalar1=-1.0,
                        scalar2=None, op0=ALU.mult)

    @chain("tensor", "sinv_bcast")
    def _(t):
        t.matmul(miscp[0:64, 5:6], onesr_f[0:1, 0:64], sc11[0:1, 6:7], start=True, stop=True)
        t.matmul(miscp[0:64, 6:7], onesr_f[0:1, 0:64], sc11[0:1, 7:8], start=True, stop=True)

    @chain("vector", "rb_el")
    def _(v):
        SI_R, SI_I = T4T5 = (W_R, W_I)
        v.tensor_copy(out=V(SI_R), in_=miscp[0:64, 5:6])
        v.tensor_copy(out=V(SI_I), in_=miscp[0:64, 6:7])
        cmul_vec(v, T2, T3, R_R, R_I, B_R, B_I)

    @chain("tensor", "rb_sum")
    def _(t):
        t.matmul(miscp[0:1, 7:8], ones_f[0:64, 0:1], vec[:, T2:T2 + 1], start=True, stop=True)
        t.matmul(miscp[0:1, 8:9], ones_f[0:64, 0:1], vec[:, T3:T3 + 1], start=True, stop=True)

    @chain("vector", "rb_scal")
    def _(v):
        v.tensor_copy(out=sc11[0:1, 0:1], in_=miscp[0:1, 7:8])
        v.tensor_copy(out=sc11[0:1, 1:2], in_=miscp[0:1, 8:9])
        v.tensor_tensor(out=sc11[0:1, 2:3], in0=sc11[0:1, 0:1], in1=sc11[0:1, 6:7], op=ALU.mult)
        v.tensor_tensor(out=sc11[0:1, 3:4], in0=sc11[0:1, 1:2], in1=sc11[0:1, 7:8], op=ALU.mult)
        v.tensor_tensor(out=sc11[0:1, 4:5], in0=sc11[0:1, 2:3], in1=sc11[0:1, 3:4], op=ALU.subtract)
        v.tensor_tensor(out=sc11[0:1, 2:3], in0=sc11[0:1, 0:1], in1=sc11[0:1, 7:8], op=ALU.mult)
        v.tensor_tensor(out=sc11[0:1, 3:4], in0=sc11[0:1, 1:2], in1=sc11[0:1, 6:7], op=ALU.mult)
        v.tensor_tensor(out=sc11[0:1, 5:6], in0=sc11[0:1, 2:3], in1=sc11[0:1, 3:4], op=ALU.add)

    @chain("tensor", "rb_bcast")
    def _(t):
        t.matmul(sparep[0:64, 0:1], onesr_f[0:1, 0:64], sc11[0:1, 4:5], start=True, stop=True)
        t.matmul(sparep[0:64, 1:2], onesr_f[0:1, 0:64], sc11[0:1, 5:6], start=True, stop=True)

    @chain("vector", "btil")
    def _(v):
        CR, CI = T4, T5 = T0, T1
        v.tensor_copy(out=V(T2), in_=sparep[0:64, 0:1])
        v.tensor_copy(out=V(T3), in_=sparep[0:64, 1:2])
        v.tensor_tensor(out=V(T0), in0=V(A_R), in1=V(T2), op=ALU.mult)
        v.tensor_tensor(out=V(T1), in0=V(A_I), in1=V(T3), op=ALU.mult)
        v.tensor_tensor(out=V(W_R), in0=V(T0), in1=V(T1), op=ALU.subtract)
        v.tensor_tensor(out=V(T0), in0=V(A_R), in1=V(T3), op=ALU.mult)
        v.tensor_tensor(out=V(T1), in0=V(A_I), in1=V(T2), op=ALU.mult)
        v.tensor_tensor(out=V(W_I), in0=V(T0), in1=V(T1), op=ALU.add)
        cmul_vec(v, T2, T3, DMI_R, DMI_I, B_R, B_I)
        v.tensor_tensor(out=V(BT_R), in0=V(T2), in1=V(W_R), op=ALU.subtract)
        v.tensor_tensor(out=V(BT_I), in0=V(T3), in1=V(W_I), op=ALU.subtract)
        cmul_vec(v, G_R, G_I, DMI_R, DMI_I, DP_R, DP_I)
        cmul_vec(v, T2, T3, R_R, R_I, U_R, U_I)

    @chain("tensor", "rho_sum")
    def _(t):
        t.matmul(sparep[0:1, 2:3], ones_f[0:64, 0:1], vec[:, T2:T2 + 1], start=True, stop=True)
        t.matmul(sparep[0:1, 3:4], ones_f[0:64, 0:1], vec[:, T3:T3 + 1], start=True, stop=True)

    @chain("vector", "kappa")
    def _(v):
        v.tensor_copy(out=sc11[0:1, 0:1], in_=sparep[0:1, 2:3])
        v.tensor_copy(out=sc11[0:1, 1:2], in_=sparep[0:1, 3:4])
        v.tensor_tensor(out=sc11[0:1, 2:3], in0=sc11[0:1, 0:1], in1=sc11[0:1, 6:7], op=ALU.mult)
        v.tensor_tensor(out=sc11[0:1, 3:4], in0=sc11[0:1, 1:2], in1=sc11[0:1, 7:8], op=ALU.mult)
        v.tensor_tensor(out=sc11[0:1, 4:5], in0=sc11[0:1, 2:3], in1=sc11[0:1, 3:4], op=ALU.subtract)
        v.tensor_scalar(out=sc11[0:1, 4:5], in0=sc11[0:1, 4:5], scalar1=-1.0,
                        scalar2=None, op0=ALU.add)
        v.tensor_tensor(out=sc11[0:1, 2:3], in0=sc11[0:1, 0:1], in1=sc11[0:1, 7:8], op=ALU.mult)
        v.tensor_tensor(out=sc11[0:1, 3:4], in0=sc11[0:1, 1:2], in1=sc11[0:1, 6:7], op=ALU.mult)
        v.tensor_tensor(out=sc11[0:1, 5:6], in0=sc11[0:1, 2:3], in1=sc11[0:1, 3:4], op=ALU.add)

    @chain("tensor", "kap_bcast")
    def _(t):
        t.matmul(sparep[0:64, 4:5], onesr_f[0:1, 0:64], sc11[0:1, 4:5], start=True, stop=True)
        t.matmul(sparep[0:64, 5:6], onesr_f[0:1, 0:64], sc11[0:1, 5:6], start=True, stop=True)

    @chain("vector", "mcol")
    def _(v):
        KR, KI = T4, T5 = (T2, T3)
        v.tensor_copy(out=V(T2), in_=sparep[0:64, 4:5])
        v.tensor_copy(out=V(T3), in_=sparep[0:64, 5:6])
        v.tensor_scalar(out=V(T0), in0=V(Q_I), scalar1=-1.0, scalar2=None, op0=ALU.mult)
        v.tensor_tensor(out=V(M_R), in0=V(T2), in1=V(Q_R), op=ALU.mult)
        v.tensor_tensor(out=V(T1), in0=V(T3), in1=V(T0), op=ALU.mult)
        v.tensor_tensor(out=V(M_R), in0=V(M_R), in1=V(T1), op=ALU.subtract)
        v.tensor_tensor(out=V(M_I), in0=V(T2), in1=V(T0), op=ALU.mult)
        v.tensor_tensor(out=V(T1), in0=V(T3), in1=V(Q_R), op=ALU.mult)
        v.tensor_tensor(out=V(M_I), in0=V(M_I), in1=V(T1), op=ALU.add)
        cmul_vec(v, T2, T3, R_R, R_I, DP_R, DP_I)
        v.tensor_copy(out=V(W_R), in_=miscp[0:64, 5:6])
        v.tensor_copy(out=V(W_I), in_=miscp[0:64, 6:7])
        v.tensor_tensor(out=V(T0), in0=V(W_R), in1=V(T2), op=ALU.mult)
        v.tensor_tensor(out=V(T1), in0=V(W_I), in1=V(T3), op=ALU.mult)
        v.tensor_tensor(out=V(T0), in0=V(T0), in1=V(T1), op=ALU.subtract)
        v.tensor_tensor(out=V(M_R), in0=V(M_R), in1=V(T0), op=ALU.subtract)
        v.tensor_tensor(out=V(T0), in0=V(W_R), in1=V(T3), op=ALU.mult)
        v.tensor_tensor(out=V(T1), in0=V(W_I), in1=V(T2), op=ALU.mult)
        v.tensor_tensor(out=V(T0), in0=V(T0), in1=V(T1), op=ALU.add)
        v.tensor_tensor(out=V(M_I), in0=V(M_I), in1=V(T0), op=ALU.subtract)

    @chain("tensor", "arow_t")
    def _(t):
        t.transpose(sparep[0:1, 64:128], vec[:, A_R:A_R + 1], i64f[:, :])
        t.transpose(sparep[0:1, 128:192], vec[:, A_I:A_I + 1], i64f[:, :])
        t.transpose(sparep[0:1, 192:256], vec[:, M_R:M_R + 1], i64f[:, :])
        t.transpose(sparep[0:1, 256:320], vec[:, M_I:M_I + 1], i64f[:, :])

    @chain("vector", "arow_cp", serial=False)
    def _(v):
        v.tensor_copy(out=arow[0:1, 0:64], in_=sparep[0:1, 64:128])
        v.tensor_copy(out=arow[0:1, 64:128], in_=sparep[0:1, 128:192])
        v.tensor_copy(out=mrow[0:1, 0:64], in_=sparep[0:1, 192:256])
        v.tensor_copy(out=mrow[0:1, 64:128], in_=sparep[0:1, 256:320])
        v.tensor_scalar(out=arow[0:1, 128:192], in0=sparep[0:1, 128:192],
                        scalar1=-1.0, scalar2=None, op0=ALU.mult)
        v.tensor_scalar(out=arow[0:1, 192:256], in0=sparep[0:1, 256:320],
                        scalar1=-1.0, scalar2=None, op0=ALU.mult)

    @chain("tensor", "atil_mm")
    def _(t):
        t.matmul(miscp[0:64, 0:64], arow[0:1, 0:64], mrow[0:1, 0:64], start=True, stop=False)
        t.matmul(miscp[0:64, 0:64], arow[0:1, 128:192], mrow[0:1, 64:128], start=False, stop=True)
        t.matmul(miscp[0:64, 64:128], arow[0:1, 0:64], mrow[0:1, 64:128], start=True, stop=False)
        t.matmul(miscp[0:64, 64:128], arow[0:1, 64:128], mrow[0:1, 0:64], start=False, stop=True)
        t.matmul(miscp[0:64, 128:192], mrow[0:1, 0:64], arow[0:1, 0:64], start=True, stop=False)
        t.matmul(miscp[0:64, 128:192], arow[0:1, 192:256], arow[0:1, 64:128], start=False, stop=True)
        t.matmul(miscp[0:64, 192:256], mrow[0:1, 0:64], arow[0:1, 64:128], start=True, stop=False)
        t.matmul(miscp[0:64, 192:256], mrow[0:1, 64:128], arow[0:1, 0:64], start=False, stop=True)

    @chain("vector", "atil", serial=False)
    def _(v):
        v.tensor_copy(out=MR[0][:, :], in_=miscp[0:64, 0:64])
        v.tensor_copy(out=MI[0][:, :], in_=miscp[0:64, 64:128])
        v.tensor_copy(out=MTR[0][:, :], in_=miscp[0:64, 128:192])
        v.tensor_copy(out=MTI[0][:, :], in_=miscp[0:64, 192:256])
        v.tensor_scalar(out=Tk32[0:64, 0:64], in0=i64f[:, :],
                        scalar1=vec[0:64, G_R:G_R + 1], scalar2=None, op0=ALU.mult)
        v.tensor_scalar(out=Tk32[0:64, 64:128], in0=i64f[:, :],
                        scalar1=vec[0:64, G_I:G_I + 1], scalar2=None, op0=ALU.mult)
        v.tensor_tensor(out=MR[0][:, :], in0=MR[0][:, :], in1=Tk32[0:64, 0:64], op=ALU.add)
        v.tensor_tensor(out=MI[0][:, :], in0=MI[0][:, :], in1=Tk32[0:64, 64:128], op=ALU.add)
        v.tensor_tensor(out=MTR[0][:, :], in0=MTR[0][:, :], in1=Tk32[0:64, 0:64], op=ALU.add)
        v.tensor_tensor(out=MTI[0][:, :], in0=MTI[0][:, :], in1=Tk32[0:64, 64:128], op=ALU.add)
        v.tensor_scalar(out=negscr[:, :], in0=MTI[0][:, :], scalar1=-1.0, scalar2=None, op0=ALU.mult)
        v.tensor_scalar(out=negscr2[:, :], in0=MI[0][:, :], scalar1=-1.0, scalar2=None, op0=ALU.mult)

    def sq_mm(t, k):
        t.matmul(miscp[0:64, 0:64], MTR[k][:, :], MR[k][:, :], start=True, stop=False)
        t.matmul(miscp[0:64, 0:64], negscr[:, :], MI[k][:, :], start=False, stop=True)
        t.matmul(miscp[0:64, 64:128], MTR[k][:, :], MI[k][:, :], start=True, stop=False)
        t.matmul(miscp[0:64, 64:128], MTI[k][:, :], MR[k][:, :], start=False, stop=True)
        t.matmul(miscp[0:64, 128:192], MR[k][:, :], MTR[k][:, :], start=True, stop=False)
        t.matmul(miscp[0:64, 128:192], negscr2[:, :], MTI[k][:, :], start=False, stop=True)
        t.matmul(miscp[0:64, 192:256], MR[k][:, :], MTI[k][:, :], start=True, stop=False)
        t.matmul(miscp[0:64, 192:256], MI[k][:, :], MTR[k][:, :], start=False, stop=True)

    def sq_evac(v, kk, extra=None):
        v.tensor_copy(out=MI[kk][:, :], in_=miscp[0:64, 64:128])
        v.tensor_copy(out=MTI[kk][:, :], in_=miscp[0:64, 192:256])
        v.tensor_copy(out=MR[kk][:, :], in_=miscp[0:64, 0:64])
        v.tensor_copy(out=MTR[kk][:, :], in_=miscp[0:64, 128:192])
        v.tensor_scalar(out=negscr2[:, :], in0=MI[kk][:, :], scalar1=-1.0, scalar2=None, op0=ALU.mult)
        v.tensor_scalar(out=negscr[:, :], in0=MTI[kk][:, :], scalar1=-1.0, scalar2=None, op0=ALU.mult)
        if extra:
            extra(v)

    def mk_square(k, kk, extra=None):
        @chain("tensor")
        def _(t):
            sq_mm(t, k)

        @chain("vector", serial=False)
        def _(v):
            sq_evac(v, kk, extra)

    def _negm7(v):
        v.tensor_scalar(out=gt[0:64, 0:64], in0=MTI[7][:, :], scalar1=-1.0,
                        scalar2=None, op0=ALU.mult)

    for k in range(6):
        mk_square(k, k + 1)
    mk_square(6, 7, extra=_negm7)
    mk_square(7, 8)
    mk_square(8, 9)
    mk_square(9, 8)
    mk_square(8, 9)

    def _winit(v):
        v.tensor_copy(out=V(W_R), in_=V(CT_R))
        v.tensor_copy(out=V(W_I), in_=V(CT_I))
    mk_square(9, 8, extra=_winit)

    def vprod_step(slot, do_square):
        @chain("tensor")
        def _(t):
            t.matmul(miscp[0:64, 320:321], MR[slot][:, :], vec[:, W_R:W_R + 1], start=True, stop=False)
            t.matmul(miscp[0:64, 320:321], MI[slot][:, :], vec[:, W_I:W_I + 1], start=False, stop=True)
            t.matmul(miscp[0:64, 321:322], MR[slot][:, :], vec[:, W_I:W_I + 1], start=True, stop=False)
            t.matmul(miscp[0:64, 321:322], negscr2[:, :], vec[:, W_R:W_R + 1], start=False, stop=True)
            if do_square:
                sq_mm(t, slot)

        @chain("vector", serial=False)
        def _(v):
            v.tensor_tensor(out=V(W_R), in0=V(W_R), in1=miscp[0:64, 320:321], op=ALU.add)
            v.tensor_tensor(out=V(W_I), in0=V(W_I), in1=miscp[0:64, 321:322], op=ALU.add)
            if do_square:
                sq_evac(v, 9 if slot == 8 else 8)

    vprod_step(8, True)
    vprod_step(9, True)
    vprod_step(8, True)
    vprod_step(9, False)

    @chain("vector", "v_final")
    def _(v):
        v.tensor_tensor(out=V(V_R), in0=V(STP64), in1=V(W_R), op=ALU.mult)
        v.tensor_tensor(out=V(V_I), in0=V(STP64), in1=V(W_I), op=ALU.mult)
        v.tensor_copy(out=Pr[:, 127:128], in_=V(BT_R))
        v.tensor_copy(out=Pi[:, 127:128], in_=V(BT_I))

    @chain("vector", "w0neg")
    def _(v):
        v.tensor_scalar(out=negscr2[:, :], in0=MI[0][:, :], scalar1=-1.0, scalar2=None, op0=ALU.mult)

    @chain("tensor", "w0mm")
    def _(t):
        t.matmul(miscp[0:64, 320:321], MR[0][:, :], vec[:, V_R:V_R + 1], start=True, stop=False)
        t.matmul(miscp[0:64, 320:321], MI[0][:, :], vec[:, V_I:V_I + 1], start=False, stop=True)
        t.matmul(miscp[0:64, 321:322], MR[0][:, :], vec[:, V_I:V_I + 1], start=True, stop=False)
        t.matmul(miscp[0:64, 321:322], negscr2[:, :], vec[:, V_R:V_R + 1], start=False, stop=True)

    @chain("vector", "w0cp", serial=False)
    def _(v):
        v.tensor_copy(out=Wr[:, 0:1], in_=miscp[0:64, 320:321])
        v.tensor_copy(out=Wi[:, 0:1], in_=miscp[0:64, 321:322])

    def pw_double(k):
        n = 1 << k
        lo = P - 2 * n

        @chain("vector", serial=False)
        def _(v):
            v.tensor_scalar(out=negscr[:, :], in0=MTI[k][:, :], scalar1=-1.0, scalar2=None, op0=ALU.mult)
            v.tensor_scalar(out=negscr2[:, :], in0=MI[k][:, :], scalar1=-1.0, scalar2=None, op0=ALU.mult)

        @chain("tensor")
        def _(t):
            t.matmul(miscp[0:64, 0:n], MTR[k][:, :], Pr[:, P - n:P], start=True, stop=False)
            t.matmul(miscp[0:64, 0:n], negscr[:, :], Pi[:, P - n:P], start=False, stop=True)
            t.matmul(miscp[0:64, 128:128 + n], MTR[k][:, :], Pi[:, P - n:P], start=True, stop=False)
            t.matmul(miscp[0:64, 128:128 + n], MTI[k][:, :], Pr[:, P - n:P], start=False, stop=True)
            t.matmul(miscp[0:64, 256:256 + n], MR[k][:, :], Wr[:, 0:n], start=True, stop=False)
            t.matmul(miscp[0:64, 256:256 + n], MI[k][:, :], Wi[:, 0:n], start=False, stop=True)
            t.matmul(miscp[0:64, 384:384 + n], MR[k][:, :], Wi[:, 0:n], start=True, stop=False)
            t.matmul(miscp[0:64, 384:384 + n], negscr2[:, :], Wr[:, 0:n], start=False, stop=True)

        @chain("vector", serial=False)
        def _(v):
            v.tensor_copy(out=Pr[:, lo:lo + n], in_=miscp[0:64, 0:n])
            v.tensor_copy(out=Pi[:, lo:lo + n], in_=miscp[0:64, 128:128 + n])
            v.tensor_copy(out=Wr[:, n:2 * n], in_=miscp[0:64, 256:256 + n])
            v.tensor_copy(out=Wi[:, n:2 * n], in_=miscp[0:64, 384:384 + n])

    for k in range(7):
        pw_double(k)

    @chain("vector", "k0_el")
    def _(v):
        v.tensor_tensor(out=V(T0), in0=V(V_R), in1=V(BT_R), op=ALU.mult)
        v.tensor_tensor(out=V(T1), in0=V(V_I), in1=V(BT_I), op=ALU.mult)
        v.tensor_tensor(out=V(T2), in0=V(T0), in1=V(T1), op=ALU.add)

    @chain("tensor", "kvec_mm")
    def _(t):
        t.matmul(miscp[:, 0:1], Wr[:, :], vec[:, BT_R:BT_R + 1], start=True, stop=False)
        t.matmul(miscp[:, 0:1], Wi[:, :], vec[:, BT_I:BT_I + 1], start=False, stop=True)
        t.matmul(miscp[0:1, 2:3], ones_f[0:64, 0:1], vec[:, T2:T2 + 1], start=True, stop=True)

    @chain("vector", "kcol_cp")
    def _(v):
        v.tensor_copy(out=kcol[:, 0:1], in_=miscp[:, 0:1])
        v.tensor_tensor(out=Krow[0:1, 0:1], in0=miscp[0:1, 2:3], in1=dD1[0:1, 0:1], op=ALU.add)

    @chain("tensor", "krow_mm")
    def _(t):
        t.transpose(miscp[0:1, 4:4 + P], kcol[:, 0:1], i128f[:, :])

    @chain("vector", "krow")
    def _(v):
        v.tensor_copy(out=Krow[0:1, 1:P], in_=miscp[0:1, 4:4 + 127])
        v.memset(Tkb[:, :], 0.0)
        v.tensor_copy(out=Tkb[0:1, 0:P], in_=Krow[0:1, 0:P])

    @chain("tensor", "pq_t")
    def _(t):
        t.transpose(miscp[0:P, 0:64], Pr[:, :], i64f[:, :])
        t.transpose(miscp[0:P, 64:128], Pi[:, :], i64f[:, :])

    @chain("vector", "pbig")
    def _(v):
        v.tensor_copy(out=Pbig[:, 0:64], in_=miscp[0:P, 0:64])
        v.tensor_copy(out=Pbig[:, 64:128], in_=miscp[0:P, 64:128])

    @free("gpsimd", name="qm_dma")
    def _(g):
        g.wait_ge(ch, CH_IDX["k0_el"])
        pairs = [
            (Qbig[0:64, :], Wr[:, :]),
            (Qbig[64:128, :], Wi[:, :]),
            (Tk32[0:64, 0:64], MTR[7][:, :]),
            (Tk32[64:128, 64:128], MTR[7][:, :]),
            (Tk32[0:64, 64:128], MTI[7][:, :]),
            (Tk32[64:128, 0:64], gt[0:64, 0:64]),
        ]
        for dst, src in pairs:
            g.dma_start(out=dst, in_=src).then_inc(d_tk, 16)

    @free("gpsimd", name="tk_double")
    def _(g):
        g.wait_ge(ch, CH_IDX["krow"] + 1)
        for k in range(7):
            s0 = 1 << k
            g.wait_ge(d_tk, 96 + 16 * k)
            g.dma_start(out=Tkb[s0:2 * s0, s0:P], in_=Tkb[0:s0, 0:P - s0]).then_inc(d_tk, 16)

    @chain("vector", "qm_cp")
    def _(v):
        v.wait_ge(d_tk, 96)
        v.tensor_copy(out=Mhi[:, :], in_=Tk32[:, :])
        v.drain()
        v.tensor_tensor(out=Mlo[:, :], in0=Tk32[:, :], in1=Mhi[:, :], op=ALU.subtract)

    if DBG:
        @chain("gpsimd", "dbg_early")
        def _(g):
            pairs = [
                (d_dbg.ap()[:, 6100:6228], Mhi[:, :]),
                (d_dbg.ap()[:, 6228:6356], Qbig[:, :]),
                (d_dbg.ap()[0:64, 6356:6484], selhi[:, :]),
                (d_dbg.ap()[0:64, 6484:6548], MTR[7][:, :]),
                (d_dbg.ap()[0:64, 6548:6612], MTI[7][:, :]),
                (d_dbg.ap()[0:64, 6612:6676], negscr[:, :]),
                (d_dbg.ap()[0:64, 6676:6740], i64f[:, :]),
                (d_dbg.ap()[:, 6740:6868], Pbig[:, :]),
                (d_dbg.ap()[:, 6868:6996], Mlo[:, :]),
                (d_dbg.ap()[0:64, 6996:7060], MR[0][:, :]),
                (d_dbg.ap()[0:64, 7060:7124], MI[0][:, :]),
                (d_dbg.ap()[0:64, 7124:7188], MTR[0][:, :]),
                (d_dbg.ap()[0:64, 7188:7252], MR[1][:, :]),
                (d_dbg.ap()[0:64, 7252:7316], MTR[1][:, :]),
                (d_dbg.ap()[0:1, 7316:7572], arow[0:1, :]),
                (d_dbg.ap()[0:1, 7572:7700], mrow[0:1, :]),
                (d_dbg.ap()[0:64, 7700:7740], vec[:, :]),
                (d_dbg.ap()[0:64, 7740:7868], Wr[:, :]),
                (d_dbg.ap()[0:64, 7868:7996], Pr[:, :]),
            ]
            n = 0
            for dst, src in pairs:
                g.dma_start(out=dst, in_=src, allow_slow_non_contiguous=True).then_inc(d_tk, 16)
                n += 1
            g.wait_ge(d_tk, 16 * (7 + n))

    @chain("vector", "setup_end")
    def _(v):
        pass

    def mk_stat_unit(u):
        s_ = u % NS
        rnd = u // NS

        @free("tensor", name=f"stu{u}")
        def _(t):
            if u == NS:
                t.wait_ge(ch, CH_IDX["rowsA"] + 1)
            t.wait_ge(d_xbf, 16 * (s_ + 1))
            t.wait_ge(v_sq, u + 1)
            buf = tmp_sc if u % 2 == 0 else tmp_bf
            ta, tb = (0, 1) if rnd == 0 else (2, 3)
            t.matmul(ob[0][0:1, :], ones_bf[:, 0:1], xn_ap(s_, ta * TS, TS),
                     start=(s_ == 0), stop=(s_ == NS - 1))
            t.matmul(ob[1][0:1, :], ones_bf[:, 0:1], buf[:, ta * TS:(ta + 1) * TS],
                     start=(s_ == 0), stop=(s_ == NS - 1))
            t.matmul(ob[2][0:1, :], ones_bf[:, 0:1], xn_ap(s_, tb * TS, TS),
                     start=(s_ == 0), stop=(s_ == NS - 1))
            t.matmul(ob[3][0:1, :], ones_bf[:, 0:1], buf[:, tb * TS:(tb + 1) * TS],
                     start=(s_ == 0), stop=(s_ == NS - 1)).then_inc(t_st, 1)

    for u in range(2 * NS):
        mk_stat_unit(u)

    def mk_squares(rnd):
        @free("scalar", name=f"sq{rnd}")
        def _(sc):
            for uu in range(NS):
                u = rnd * NS + uu
                s_ = u % NS
                sc.wait_ge(d_xbf, 16 * (s_ + 1))
                if u >= 2:
                    sc.wait_ge(t_st, u - 1)
                buf = tmp_sc if u % 2 == 0 else tmp_bf
                sc.activation(out=buf[:, :], in_=actbuf[:, s_ * T:(s_ + 1) * T],
                              func=AF.Square).then_inc(v_sq, 1)

    mk_squares(0)
    mk_squares(1)

    @chain("vector", "rowsA", serial=False)
    def _(v):
        v.wait_ge(t_st, NS)
        v.tensor_scalar(out=scr_a[0:1, 0:TS], in0=ob[0][0:1, :],
                        scalar1=1.0 / DM, scalar2=None, op0=ALU.mult)
        v.tensor_scalar(out=scr_a[0:1, 2 * TS:3 * TS], in0=ob[2][0:1, :],
                        scalar1=1.0 / DM, scalar2=None, op0=ALU.mult)
        v.tensor_scalar(out=scr_a[0:1, TS:2 * TS], in0=ob[1][0:1, :],
                        scalar1=1.0 / DM, scalar2=None, op0=ALU.mult)
        v.tensor_scalar(out=scr_a[0:1, 3 * TS:4 * TS], in0=ob[3][0:1, :],
                        scalar1=1.0 / DM, scalar2=None, op0=ALU.mult)
        v.drain()
        v.tensor_tensor(out=rowscr[0:1, 0:TS], in0=scr_a[0:1, 0:TS],
                        in1=scr_a[0:1, 0:TS], op=ALU.mult)
        v.tensor_tensor(out=rowscr[0:1, TS:2 * TS], in0=scr_a[0:1, 2 * TS:3 * TS],
                        in1=scr_a[0:1, 2 * TS:3 * TS], op=ALU.mult)
        v.drain()
        v.tensor_tensor(out=scr_a[0:1, TS:2 * TS], in0=scr_a[0:1, TS:2 * TS],
                        in1=rowscr[0:1, 0:TS], op=ALU.subtract)
        v.tensor_tensor(out=scr_a[0:1, 3 * TS:4 * TS], in0=scr_a[0:1, 3 * TS:4 * TS],
                        in1=rowscr[0:1, TS:2 * TS], op=ALU.subtract)
        v.drain()
        v.tensor_scalar(out=scr_a[0:1, TS:2 * TS], in0=scr_a[0:1, TS:2 * TS],
                        scalar1=1e-5, scalar2=None, op0=ALU.add)
        v.tensor_scalar(out=scr_a[0:1, 3 * TS:4 * TS], in0=scr_a[0:1, 3 * TS:4 * TS],
                        scalar1=1e-5, scalar2=None, op0=ALU.add)

    @chain("scalar", "lnA", serial=False)
    def _(sc):
        sc.activation(out=rowscr[0:1, 0:TS], in_=scr_a[0:1, TS:2 * TS], func=AF.Ln)
        sc.activation(out=rowscr[0:1, TS:2 * TS], in_=scr_a[0:1, 3 * TS:4 * TS], func=AF.Ln)
        sc.drain()
        sc.activation(out=scr_a[0:1, TS:2 * TS], in_=rowscr[0:1, 0:TS], func=AF.Exp, scale=-0.5)
        sc.activation(out=scr_a[0:1, 3 * TS:4 * TS], in_=rowscr[0:1, TS:2 * TS], func=AF.Exp, scale=-0.5)

    @chain("vector", "mursA", serial=False)
    def _(v):
        v.tensor_tensor(out=rowscr[0:1, 0:TS], in0=scr_a[0:1, 0:TS],
                        in1=scr_a[0:1, TS:2 * TS], op=ALU.mult)
        v.tensor_tensor(out=rowscr[0:1, TS:2 * TS], in0=scr_a[0:1, 2 * TS:3 * TS],
                        in1=scr_a[0:1, 3 * TS:4 * TS], op=ALU.mult)

    @chain("vector", "rowsB", serial=False)
    def _(v):
        v.wait_ge(t_st, 2 * NS)
        v.tensor_scalar(out=tmp_sc[0:1, 2 * TS:3 * TS], in0=ob[0][0:1, :],
                        scalar1=1.0 / DM, scalar2=None, op0=ALU.mult)
        v.tensor_scalar(out=tmp_sc[0:1, 3 * TS:4 * TS], in0=ob[2][0:1, :],
                        scalar1=1.0 / DM, scalar2=None, op0=ALU.mult)
        v.tensor_scalar(out=tmp_sc[0:1, 0:TS], in0=ob[1][0:1, :],
                        scalar1=1.0 / DM, scalar2=None, op0=ALU.mult)
        v.tensor_scalar(out=tmp_sc[0:1, TS:2 * TS], in0=ob[3][0:1, :],
                        scalar1=1.0 / DM, scalar2=None, op0=ALU.mult)
        v.drain()
        v.tensor_tensor(out=tmp_bf[0:1, 0:TS], in0=tmp_sc[0:1, 2 * TS:3 * TS],
                        in1=tmp_sc[0:1, 2 * TS:3 * TS], op=ALU.mult)
        v.tensor_tensor(out=tmp_bf[0:1, TS:2 * TS], in0=tmp_sc[0:1, 3 * TS:4 * TS],
                        in1=tmp_sc[0:1, 3 * TS:4 * TS], op=ALU.mult)
        v.drain()
        v.tensor_tensor(out=tmp_sc[0:1, 0:TS], in0=tmp_sc[0:1, 0:TS],
                        in1=tmp_bf[0:1, 0:TS], op=ALU.subtract)
        v.tensor_tensor(out=tmp_sc[0:1, TS:2 * TS], in0=tmp_sc[0:1, TS:2 * TS],
                        in1=tmp_bf[0:1, TS:2 * TS], op=ALU.subtract)
        v.drain()
        v.tensor_scalar(out=tmp_sc[0:1, 0:TS], in0=tmp_sc[0:1, 0:TS],
                        scalar1=1e-5, scalar2=None, op0=ALU.add)
        v.tensor_scalar(out=tmp_sc[0:1, TS:2 * TS], in0=tmp_sc[0:1, TS:2 * TS],
                        scalar1=1e-5, scalar2=None, op0=ALU.add)

    @chain("scalar", "lnB", serial=False)
    def _(sc):
        sc.activation(out=tmp_bf[0:1, 0:TS], in_=tmp_sc[0:1, 0:TS], func=AF.Ln)
        sc.activation(out=tmp_bf[0:1, TS:2 * TS], in_=tmp_sc[0:1, TS:2 * TS], func=AF.Ln)
        sc.drain()
        sc.activation(out=scr_a[0:1, 0:TS], in_=tmp_bf[0:1, 0:TS], func=AF.Exp, scale=-0.5)
        sc.activation(out=scr_a[0:1, 2 * TS:3 * TS], in_=tmp_bf[0:1, TS:2 * TS], func=AF.Exp, scale=-0.5)

    @chain("vector", "mursB", serial=False)
    def _(v):
        v.tensor_tensor(out=tmp_bf[0:1, 0:TS], in0=tmp_sc[0:1, 2 * TS:3 * TS],
                        in1=scr_a[0:1, 0:TS], op=ALU.mult)
        v.tensor_tensor(out=tmp_bf[0:1, TS:2 * TS], in0=tmp_sc[0:1, 3 * TS:4 * TS],
                        in1=scr_a[0:1, 2 * TS:3 * TS], op=ALU.mult)

    @chain("tensor", "mapmmA")
    def _(t):
        t.matmul(ob[0][:, :], onesr_bf[0:1, 0:P], rowscr[0:1, 0:TS], start=True, stop=True)
        t.matmul(ob[1][:, :], onesr_f[0:1, 0:P], scr_a[0:1, TS:2 * TS], start=True, stop=True)
        t.matmul(ob[2][:, :], onesr_bf[0:1, 0:P], rowscr[0:1, TS:2 * TS], start=True, stop=True)
        t.matmul(ob[3][:, :], onesr_f[0:1, 0:P], scr_a[0:1, 3 * TS:4 * TS], start=True, stop=True)

    @chain("vector", "mapcpA", serial=False)
    def _(v):
        v.tensor_copy(out=map_mu[:, 0:TS], in_=ob[0][:, :])
        v.tensor_copy(out=map_rs[:, 0:TS], in_=ob[1][:, :])
        v.tensor_copy(out=map_mu[:, TS:2 * TS], in_=ob[2][:, :])
        v.tensor_copy(out=map_rs[:, TS:2 * TS], in_=ob[3][:, :])

    @chain("tensor", "mapmmB")
    def _(t):
        t.matmul(ob[0][:, :], onesr_bf[0:1, 0:P], tmp_bf[0:1, 0:TS], start=True, stop=True)
        t.matmul(ob[1][:, :], onesr_f[0:1, 0:P], scr_a[0:1, 0:TS], start=True, stop=True)
        t.matmul(ob[2][:, :], onesr_bf[0:1, 0:P], tmp_bf[0:1, TS:2 * TS], start=True, stop=True)
        t.matmul(ob[3][:, :], onesr_f[0:1, 0:P], scr_a[0:1, 2 * TS:3 * TS], start=True, stop=True)

    @chain("vector", "mapcpB", serial=False)
    def _(v):
        v.tensor_copy(out=map_mu[:, 2 * TS:3 * TS], in_=ob[0][:, :])
        v.tensor_copy(out=map_rs[:, 2 * TS:3 * TS], in_=ob[1][:, :])
        v.tensor_copy(out=map_mu[:, 3 * TS:4 * TS], in_=ob[2][:, :])
        v.tensor_copy(out=map_rs[:, 3 * TS:4 * TS], in_=ob[3][:, :])

    @free("vector")
    def _(v):
        for s in range(NS):
            v.tensor_tensor(out=tmp_sc[:, :], in0=actbuf[:, s * T:(s + 1) * T],
                            in1=map_rs[:, :], op=ALU.mult)
            v.tensor_tensor(out=actbuf[:, s * T:(s + 1) * T], in0=tmp_sc[:, :],
                            in1=map_mu[:, :], op=ALU.subtract).then_inc(v_xn, 1)

    @free("tensor")
    def _(t):
        t.wait_ge(d_gsm, 48)
        for s in range(NS):
            t.wait_ge(v_xn, s + 1)
            for tt in range(NT):
                t.matmul(ob[tt][:, :], lAx[:, s * P:(s + 1) * P],
                         xn_ap(s, tt * TS, TS), start=(s == 0), stop=(s == NS - 1))
            for tt in range(NT):
                t.matmul(hbanks[tt][0:32, :], g1b[:, s * 32:(s + 1) * 32],
                         xn_ap(s, tt * TS, TS), start=(s == 0), stop=(s == NS - 1))

    @chain("tensor", "iter0_done")
    def _(t):
        pass

    @chain("vector", "tmp_evac", serial=False)
    def _(v):
        for tt in range(NT):
            v.tensor_copy(out=tmp_bf[:, tt * TS:(tt + 1) * TS], in_=ob[tt][:, :])

    @chain("vector", "h_evac", serial=False)
    def _(v):
        for tt in range(NT):
            v.tensor_scalar(out=gh[0:32, tt * TS:(tt + 1) * TS], in0=hbanks[tt][0:32, :],
                            scalar1=gb1s[:, 0:1], scalar2=None, op0=ALU.add)

    @chain("vector", "hsq")
    def _(v):
        v.tensor_tensor(out=tmp_sc[0:32, :], in0=gh[0:32, :], in1=gh[0:32, :], op=ALU.mult)

    @chain("tensor", "hstat_mm")
    def _(t):
        for tt in range(NT):
            t.matmul(hbanks[tt][0:1, :], ones_bf[0:32, 0:1],
                     gh[0:32, tt * TS:(tt + 1) * TS], start=True, stop=True)
        for tt in range(NT):
            t.matmul(ob[tt][0:1, :], ones_bf[0:32, 0:1],
                     tmp_sc[0:32, tt * TS:(tt + 1) * TS], start=True, stop=True)

    @chain("vector", "hrows_r0", serial=False)
    def _(v):
        v.tensor_scalar(out=scr_a[0:1, 0:TS], in0=hbanks[0][0:1, :],
                        scalar1=1.0 / 32, scalar2=None, op0=ALU.mult)
        v.tensor_scalar(out=scr_a[0:1, 2 * TS:3 * TS], in0=hbanks[1][0:1, :],
                        scalar1=1.0 / 32, scalar2=None, op0=ALU.mult)
        v.tensor_scalar(out=scr_a[0:1, TS:2 * TS], in0=ob[0][0:1, :],
                        scalar1=1.0 / 32, scalar2=None, op0=ALU.mult)
        v.tensor_scalar(out=scr_a[0:1, 3 * TS:4 * TS], in0=ob[1][0:1, :],
                        scalar1=1.0 / 32, scalar2=None, op0=ALU.mult)
        v.drain()
        v.tensor_tensor(out=rowscr[0:1, 0:TS], in0=scr_a[0:1, 0:TS],
                        in1=scr_a[0:1, 0:TS], op=ALU.mult)
        v.tensor_tensor(out=rowscr[0:1, TS:2 * TS], in0=scr_a[0:1, 2 * TS:3 * TS],
                        in1=scr_a[0:1, 2 * TS:3 * TS], op=ALU.mult)
        v.drain()
        v.tensor_tensor(out=scr_a[0:1, TS:2 * TS], in0=scr_a[0:1, TS:2 * TS],
                        in1=rowscr[0:1, 0:TS], op=ALU.subtract)
        v.tensor_tensor(out=scr_a[0:1, 3 * TS:4 * TS], in0=scr_a[0:1, 3 * TS:4 * TS],
                        in1=rowscr[0:1, TS:2 * TS], op=ALU.subtract)
        v.drain()
        v.tensor_scalar(out=scr_a[0:1, TS:2 * TS], in0=scr_a[0:1, TS:2 * TS],
                        scalar1=1e-5, scalar2=None, op0=ALU.add)
        v.tensor_scalar(out=scr_a[0:1, 3 * TS:4 * TS], in0=scr_a[0:1, 3 * TS:4 * TS],
                        scalar1=1e-5, scalar2=None, op0=ALU.add)

    @chain("scalar", "hln_r0", serial=False)
    def _(s):
        s.activation(out=rowscr[0:1, 0:TS], in_=scr_a[0:1, TS:2 * TS], func=AF.Ln)
        s.activation(out=rowscr[0:1, TS:2 * TS], in_=scr_a[0:1, 3 * TS:4 * TS], func=AF.Ln)
        s.drain()
        s.activation(out=scr_a[0:1, TS:2 * TS], in_=rowscr[0:1, 0:TS], func=AF.Exp, scale=-0.5)
        s.activation(out=scr_a[0:1, 3 * TS:4 * TS], in_=rowscr[0:1, TS:2 * TS], func=AF.Exp, scale=-0.5)

    @chain("tensor", "hmap_r0")
    def _(t):
        t.matmul(hbanks[0][0:32, :], onesr_f[0:1, 0:32], scr_a[0:1, 0:TS], start=True, stop=True)
        t.matmul(ob[0][0:32, :], onesr_f[0:1, 0:32], scr_a[0:1, TS:2 * TS], start=True, stop=True)
        t.matmul(hbanks[1][0:32, :], onesr_f[0:1, 0:32], scr_a[0:1, 2 * TS:3 * TS], start=True, stop=True)
        t.matmul(ob[1][0:32, :], onesr_f[0:1, 0:32], scr_a[0:1, 3 * TS:4 * TS], start=True, stop=True)

    @chain("vector", "hlnv_r0", serial=False)
    def _(v):
        for tt in (0, 1):
            v.tensor_tensor(out=scr_a[0:32, tt * TS:(tt + 1) * TS],
                            in0=gh[0:32, tt * TS:(tt + 1) * TS],
                            in1=hbanks[tt][0:32, :], op=ALU.subtract)
        v.drain()
        for tt in (0, 1):
            v.tensor_tensor(out=scr_a[0:32, tt * TS:(tt + 1) * TS],
                            in0=scr_a[0:32, tt * TS:(tt + 1) * TS],
                            in1=ob[tt][0:32, :], op=ALU.mult)

    @chain("scalar", "hrelu_r0")
    def _(s):
        s.activation(out=gh[0:32, 0 * TS:(1 + 1) * TS], in_=scr_a[0:32, 0 * TS:(1 + 1) * TS],
                     func=AF.Relu, bias=lnbs[:, 0:1], scale=lngs[:, 0:1])

    @chain("vector", "hrows_r1", serial=False)
    def _(v):
        v.tensor_scalar(out=scr_a[0:1, 0:TS], in0=hbanks[2][0:1, :],
                        scalar1=1.0 / 32, scalar2=None, op0=ALU.mult)
        v.tensor_scalar(out=scr_a[0:1, 2 * TS:3 * TS], in0=hbanks[3][0:1, :],
                        scalar1=1.0 / 32, scalar2=None, op0=ALU.mult)
        v.tensor_scalar(out=scr_a[0:1, TS:2 * TS], in0=ob[2][0:1, :],
                        scalar1=1.0 / 32, scalar2=None, op0=ALU.mult)
        v.tensor_scalar(out=scr_a[0:1, 3 * TS:4 * TS], in0=ob[3][0:1, :],
                        scalar1=1.0 / 32, scalar2=None, op0=ALU.mult)
        v.drain()
        v.tensor_tensor(out=rowscr[0:1, 0:TS], in0=scr_a[0:1, 0:TS],
                        in1=scr_a[0:1, 0:TS], op=ALU.mult)
        v.tensor_tensor(out=rowscr[0:1, TS:2 * TS], in0=scr_a[0:1, 2 * TS:3 * TS],
                        in1=scr_a[0:1, 2 * TS:3 * TS], op=ALU.mult)
        v.drain()
        v.tensor_tensor(out=scr_a[0:1, TS:2 * TS], in0=scr_a[0:1, TS:2 * TS],
                        in1=rowscr[0:1, 0:TS], op=ALU.subtract)
        v.tensor_tensor(out=scr_a[0:1, 3 * TS:4 * TS], in0=scr_a[0:1, 3 * TS:4 * TS],
                        in1=rowscr[0:1, TS:2 * TS], op=ALU.subtract)
        v.drain()
        v.tensor_scalar(out=scr_a[0:1, TS:2 * TS], in0=scr_a[0:1, TS:2 * TS],
                        scalar1=1e-5, scalar2=None, op0=ALU.add)
        v.tensor_scalar(out=scr_a[0:1, 3 * TS:4 * TS], in0=scr_a[0:1, 3 * TS:4 * TS],
                        scalar1=1e-5, scalar2=None, op0=ALU.add)

    @chain("scalar", "hln_r1", serial=False)
    def _(s):
        s.activation(out=rowscr[0:1, 0:TS], in_=scr_a[0:1, TS:2 * TS], func=AF.Ln)
        s.activation(out=rowscr[0:1, TS:2 * TS], in_=scr_a[0:1, 3 * TS:4 * TS], func=AF.Ln)
        s.drain()
        s.activation(out=scr_a[0:1, TS:2 * TS], in_=rowscr[0:1, 0:TS], func=AF.Exp, scale=-0.5)
        s.activation(out=scr_a[0:1, 3 * TS:4 * TS], in_=rowscr[0:1, TS:2 * TS], func=AF.Exp, scale=-0.5)

    @chain("tensor", "hmap_r1")
    def _(t):
        t.matmul(hbanks[2][0:32, :], onesr_f[0:1, 0:32], scr_a[0:1, 0:TS], start=True, stop=True)
        t.matmul(ob[2][0:32, :], onesr_f[0:1, 0:32], scr_a[0:1, TS:2 * TS], start=True, stop=True)
        t.matmul(hbanks[3][0:32, :], onesr_f[0:1, 0:32], scr_a[0:1, 2 * TS:3 * TS], start=True, stop=True)
        t.matmul(ob[3][0:32, :], onesr_f[0:1, 0:32], scr_a[0:1, 3 * TS:4 * TS], start=True, stop=True)

    @chain("vector", "hlnv_r1", serial=False)
    def _(v):
        for tt in (2, 3):
            v.tensor_tensor(out=scr_a[0:32, tt * TS:(tt + 1) * TS],
                            in0=gh[0:32, tt * TS:(tt + 1) * TS],
                            in1=hbanks[tt][0:32, :], op=ALU.subtract)
        v.drain()
        for tt in (2, 3):
            v.tensor_tensor(out=scr_a[0:32, tt * TS:(tt + 1) * TS],
                            in0=scr_a[0:32, tt * TS:(tt + 1) * TS],
                            in1=ob[tt][0:32, :], op=ALU.mult)

    @chain("scalar", "hrelu_r1")
    def _(s):
        s.activation(out=gh[0:32, 2 * TS:(3 + 1) * TS], in_=scr_a[0:32, 2 * TS:(3 + 1) * TS],
                     func=AF.Relu, bias=lnbs[:, 0:1], scale=lngs[:, 0:1])

    @chain("tensor", "gmm_all")
    def _(t):
        for tt in range(NT):
            t.matmul(ob[tt][0:8, :], g2b[:, :], gh[0:32, tt * TS:(tt + 1) * TS],
                     start=True, stop=True)

    @chain("vector", "gev_all", serial=False)
    def _(v):
        for tt in range(NT):
            v.tensor_scalar(out=gatesf[:, tt * TS:(tt + 1) * TS], in0=ob[tt][0:8, :],
                            scalar1=gb2s[:, 0:1], scalar2=None, op0=ALU.add)

    @chain("tensor", "gt_mm")
    def _(t):
        for gix in range(16):
            t.matmul(miscp[0:P, gix * 8:(gix + 1) * 8],
                     gatesf[:, gix * P:(gix + 1) * P], i8f[:, :],
                     start=(gix == 0), stop=(gix == 15))

    @chain("vector", "gt_cp")
    def _(v):
        v.tensor_copy(out=gt[:, :], in_=miscp[0:P, 0:128])

    @chain("vector", "top2a", serial=False)
    def _(v):
        G = lambda gix: gt[:, gix * 8:(gix + 1) * 8]
        W = lambda gix: wtok[:, gix * 8:(gix + 1) * 8]
        O1 = lambda gix: oh1[:, gix * 8:(gix + 1) * 8]
        O2 = lambda gix: oh2[:, gix * 8:(gix + 1) * 8]
        for g_ in range(16):
            v.tensor_reduce(out=d16[:, g_:g_ + 1], in_=G(g_), axis=mybir.AxisListType.X, op=ALU.max)
        for g_ in range(16):
            v.tensor_scalar(out=O1(g_), in0=G(g_), scalar1=d16[:, g_:g_ + 1], scalar2=None, op0=ALU.is_equal)
        for g_ in range(16):
            v.tensor_tensor(out=W(g_), in0=O1(g_), in1=c8mi[:, :], op=ALU.mult)
        for g_ in range(16):
            v.tensor_scalar(out=W(g_), in0=W(g_), scalar1=-1.0, scalar2=8.0, op0=ALU.mult, op1=ALU.add)
        for g_ in range(16):
            v.tensor_reduce(out=e16[:, g_:g_ + 1], in_=W(g_), axis=mybir.AxisListType.X, op=ALU.min)
        for g_ in range(16):
            v.tensor_scalar(out=O1(g_), in0=iot8[:, :], scalar1=e16[:, g_:g_ + 1], scalar2=None, op0=ALU.is_equal)
        for g_ in range(16):
            v.scalar_tensor_tensor(out=W(g_), in0=O1(g_), scalar=-1e30, in1=G(g_), op0=ALU.mult, op1=ALU.add)
        for g_ in range(16):
            v.tensor_reduce(out=tg1c[:, g_:g_ + 1], in_=W(g_), axis=mybir.AxisListType.X, op=ALU.max)
        for g_ in range(16):
            v.tensor_scalar(out=O2(g_), in0=W(g_), scalar1=tg1c[:, g_:g_ + 1], scalar2=None, op0=ALU.is_equal)
        for g_ in range(16):
            v.tensor_tensor(out=W(g_), in0=O2(g_), in1=c8mi[:, :], op=ALU.mult)
        for g_ in range(16):
            v.tensor_scalar(out=W(g_), in0=W(g_), scalar1=-1.0, scalar2=8.0, op0=ALU.mult, op1=ALU.add)
        for g_ in range(16):
            v.tensor_reduce(out=tg2c[:, g_:g_ + 1], in_=W(g_), axis=mybir.AxisListType.X, op=ALU.min)
        for g_ in range(16):
            v.tensor_scalar(out=O2(g_), in0=iot8[:, :], scalar1=tg2c[:, g_:g_ + 1], scalar2=None, op0=ALU.is_equal)
        for g_ in range(16):
            v.tensor_tensor(out=d16[:, g_:g_ + 1], in0=tg1c[:, g_:g_ + 1], in1=d16[:, g_:g_ + 1], op=ALU.subtract)

    @chain("tensor", "cnt_mm")
    def _(t):
        for gix in range(16):
            t.matmul(miscp[0:8, 0:1], oh1[:, gix * 8:(gix + 1) * 8], ones_bf[:, 0:1],
                     start=(gix == 0), stop=(gix == 15))
        for gix in range(16):
            t.matmul(miscp[0:8, 2:3], oh2[:, gix * 8:(gix + 1) * 8], ones_bf[:, 0:1],
                     start=(gix == 0), stop=(gix == 15))

    @chain("vector", "cnt_cp", serial=False)
    def _(v):
        v.tensor_copy(out=cnt_sb[0:8, 0:1], in_=miscp[0:8, 0:1])
        v.tensor_copy(out=cnt_sb[0:8, 1:2], in_=miscp[0:8, 2:3])

    @chain("tensor", "cnt_t")
    def _(t):
        t.matmul(sparep[0:1, 0:8], cnt_sb[0:8, 0:1], i8f[:, :], start=True, stop=True)
        t.matmul(sparep[0:1, 8:16], cnt_sb[0:8, 1:2], i8f[:, :], start=True, stop=True)

    @chain("vector", "cc_cp")
    def _(v):
        v.tensor_copy(out=ccs[0:1, :], in_=sparep[0:1, 0:16])

    @chain("gpsimd", "cc_go")
    def _(g):
        g.dma_start(out=d_ccin.ap()[:, :], in_=ccs[0:1, :]).then_inc(d_cc, 16)
        g.wait_ge(d_cc, 16)
        g.collective_compute(
            "AllReduce", ALU.add,
            replica_groups=[list(range(NCORES))],
            ins=[d_ccin.ap()],
            outs=[d_ccout.ap()],
        ).then_inc(c_cc, 1)

    @chain("scalar", "top2exp")
    def _(s):
        s.activation(out=e16[:, :], in_=d16[:, :], func=AF.Exp)

    @chain("vector", "top2b")
    def _(v):
        v.tensor_scalar(out=tg1c[:, :], in0=e16[:, :], scalar1=1.0, scalar2=None, op0=ALU.add)
        v.reciprocal(out=tg1c[:, :], in_=tg1c[:, :])
        v.tensor_tensor(out=tg2c[:, :], in0=e16[:, :], in1=tg1c[:, :], op=ALU.mult)

    @chain("gpsimd", "cc_fin")
    def _(g):
        g.wait_ge(c_cc, 1)
        g.dma_start(out=cct[0:1, :], in_=d_ccout.ap()[:, :]).then_inc(d_cc, 16)
        g.wait_ge(d_cc, 32)

    @chain("tensor", "al_bcast")
    def _(t):
        t.matmul(miscp[:, 0:8], onesr_f[0:1, 0:P], cct[0:1, 0:8], start=True, stop=True)
        t.matmul(miscp[:, 8:16], onesr_f[0:1, 0:P], cct[0:1, 8:16], start=True, stop=True)

    @chain("vector", "wtok_fin", serial=False)
    def _(v):
        v.tensor_scalar(out=alb[:, :], in0=miscp[:, 0:16], scalar1=CAP, scalar2=None,
                        op0=ALU.is_le)
        v.drain()
        SL = lambda gix: slice(gix * 8, (gix + 1) * 8)
        for g_ in range(16):
            v.tensor_scalar(out=oh1[:, SL(g_)], in0=oh1[:, SL(g_)],
                            scalar1=tg1c[:, g_:g_ + 1], scalar2=None, op0=ALU.mult)
        for g_ in range(16):
            v.tensor_scalar(out=oh2[:, SL(g_)], in0=oh2[:, SL(g_)],
                            scalar1=tg2c[:, g_:g_ + 1], scalar2=None, op0=ALU.mult)
        for g_ in range(16):
            v.tensor_tensor(out=oh1[:, SL(g_)], in0=oh1[:, SL(g_)], in1=alb[:, 0:8], op=ALU.mult)
        for g_ in range(16):
            v.tensor_tensor(out=oh2[:, SL(g_)], in0=oh2[:, SL(g_)], in1=alb[:, 8:16], op=ALU.mult)
        for g_ in range(16):
            v.tensor_tensor(out=wtok[:, SL(g_)], in0=oh1[:, SL(g_)], in1=oh2[:, SL(g_)], op=ALU.add)

    def wexp_tt(tt):
        @chain("tensor", f"wfm{tt}")
        def _(t):
            for gg in range(4):
                gix = tt * 4 + gg
                t.matmul(sparep[0:8, gg * P:(gg + 1) * P], wtok[:, gix * 8:(gix + 1) * 8],
                         i128b[:, :], start=True, stop=True)

        @chain("vector", f"wfmcp{tt}")
        def _(v):
            v.tensor_copy(out=gh[0:8, tt * TS:(tt + 1) * TS], in_=sparep[0:8, :])

        @chain("tensor", f"wexp{tt}")
        def _(t):
            t.matmul(miscp[:, :], esel[:, :], gh[0:8, tt * TS:(tt + 1) * TS],
                     start=True, stop=True)

        @chain("vector", f"tmpsc{tt}", )
        def _(v):
            v.tensor_tensor(out=tmp_sc[:, tt * TS:(tt + 1) * TS],
                            in0=tmp_bf[:, tt * TS:(tt + 1) * TS], in1=miscp[:, :],
                            op=ALU.mult)

    for tt in range(NT):
        wexp_tt(tt)

    @chain("vector", "wexp_done")
    def _(v):
        pass

    if DBG:
        @chain("gpsimd", "dbg_fill")
        def _(g):
            pairs = [
                (d_dbg.ap()[:, 0:2048], actbuf[:, 0:2048]),
                (d_dbg.ap()[0:1, 2048:3072], rowscr[0:1, :]),
                (d_dbg.ap()[0:8, 3072:5120], gatesf[:, :]),
                (d_dbg.ap()[0:1, 5120:5121], stp1[:, :]),
                (d_dbg.ap()[0:1, 5121:5122], dD1[:, :]),
                (d_dbg.ap()[0:1, 5122:5250], Krow[0:1, :]),
                (d_dbg.ap()[:, 5250:5378], Mhi[:, :]),
                (d_dbg.ap()[:, 5378:5506], Tkb[:, :]),
                (d_dbg.ap()[:, 5506:5634], map_mu[:, 0:128]),
                (d_dbg.ap()[:, 5634:5762], map_rs[:, 0:128]),
                (d_dbg.ap()[:, 5762:5890], tmp_bf[:, 0:128]),
                (d_dbg.ap()[:, 5890:6018], tmp_sc[:, 0:128]),
                (d_dbg.ap()[0:P, 6018:6034], alb[:, :]),
                (d_dbg.ap()[0:1, 6034:6050], cct[0:1, :]),
            ]
            n = 0
            for dst, src in pairs:
                g.dma_start(out=dst, in_=src, allow_slow_non_contiguous=True).then_inc(d_tk, 16)
                n += 1
            g.wait_ge(d_tk, 16 * (16 + n))


    banks6 = [ob[0], ob[1], ob[2], ob[3], miscp, sparep]

    @free("tensor")
    def _(t):
        for f in range(NS):
            for tt in range(NT):
                g = 4 * f + tt
                bk = banks6[g % 6]
                if f == 0:
                    t.wait_ge(ch, CH_IDX["wexp_done"] + 1)
                    t.wait_ge(d_tk, 208)
                elif g - 6 >= 0:
                    t.wait_ge(s_evac, g - 6 + 1)
                if f >= 1:
                    t.wait_ge(s_split, 4 * (f - 1) + tt + 1)
                t.matmul(bk[:, :], Tkb[:, :], xn_ap(f, tt * TS, TS),
                         start=True, stop=False)
                if f >= 1:
                    t.matmul(bk[:, :], Qbig[:, :], S_hi[:, tt * TS:(tt + 1) * TS],
                             start=False, stop=False)
                if f <= NS - 2:
                    if tt >= 2:
                        t.wait_ge(s_split, 4 * f + (tt - 2) + 1)
                    sbank = sbk[tt % 2]
                    if f == 0:
                        t.matmul(sbank[:, :], Pbig[:, :], xn_ap(f, tt * TS, TS),
                                 start=True, stop=True).then_inc(s_scanm, 1)
                    else:
                        t.matmul(sbank[:, :], Mhi[:, :], S_hi[:, tt * TS:(tt + 1) * TS],
                                 start=True, stop=False)
                        t.matmul(sbank[:, :], Mhi[:, :], S_lo[:, tt * TS:(tt + 1) * TS],
                                 start=False, stop=False)
                        t.matmul(sbank[:, :], Mlo[:, :], S_hi[:, tt * TS:(tt + 1) * TS],
                                 start=False, stop=False)
                        t.matmul(sbank[:, :], Pbig[:, :], xn_ap(f, tt * TS, TS),
                                 start=False, stop=True).then_inc(s_scanm, 1)
            for si in range(NS):
                if si == 0:
                    t.wait_ge(d_w, 16 * (2 * f + 1))
                if si == 16:
                    t.wait_ge(d_w, 16 * (2 * f + 2))
                half = (2 * f + (si // 16)) % 2
                wsl = whbuf[:, half * 16 * P + (si % 16) * P: half * 16 * P + (si % 16) * P + P]
                for tt in range(NT):
                    mm = t.matmul(banks6[(4 * f + tt) % 6][:, :], wsl,
                                  xn_ap(si, tt * TS, TS), start=False, stop=False)
                    if tt == NT - 1 and (si == 15 or si == 31):
                        mm.then_inc(s_wfree, 1)
            for tt in range(NT):
                t.matmul(banks6[(4 * f + tt) % 6][:, :], lBb[:, f * P:(f + 1) * P],
                         tmp_sc[:, tt * TS:(tt + 1) * TS],
                         start=False, stop=True).then_inc(s_main, 1)

    @free("scalar")
    def _(s):
        for f in range(NS):
            if f <= NS - 2:
                for tt in range(NT):
                    s.wait_ge(s_scanm, 4 * f + tt + 1)
                    s.copy(out=S_hi[:, tt * TS:(tt + 1) * TS],
                           in_=sbk[tt % 2][:, :]).then_inc(sc_shi, 1)


    @free("vector")
    def _(v):
        for f in range(NS):
            if f <= NS - 2:
                for tt in range(NT):
                    v.wait_ge(sc_shi, 4 * f + tt + 1)
                    v.tensor_tensor(out=S_lo[:, tt * TS:(tt + 1) * TS],
                                    in0=sbk[tt % 2][:, :],
                                    in1=S_hi[:, tt * TS:(tt + 1) * TS],
                                    op=ALU.subtract).then_inc(s_split, 1)
            v.wait_ge(d_xr, 16 * (f + 1))
            if f >= 1:
                v.wait_ge(d_outS, 16 * f)
            for tt in range(NT):
                g = 4 * f + tt
                v.wait_ge(s_main, g + 1)
                v.tensor_tensor(out=scr_a[:, tt * TS:(tt + 1) * TS],
                                in0=banks6[g % 6][:, :],
                                in1=lAx2[:, tt * TS:(tt + 1) * TS],
                                op=ALU.add).then_inc(s_evac, 1)

    @free("sync")
    def _(sync):
        for f in range(NS):
            sync.wait_ge(s_evac, 4 * (f + 1))
            sync.dma_start(out=d_out.ap()[f * P:(f + 1) * P, :],
                           in_=scr_a[:, :]).then_inc(d_outS, 16)
        sync.wait_ge(d_outS, 16 * NS)

    @free("gpsimd")
    def _(g):
        wr = d_wT.ap()
        xr = d_xT.ap().rearrange("(s p) t -> s p t", p=P)
        g.dma_start(out=lAx2[:, 0:T], in_=xr[0]).then_inc(d_xr, 16)
        for H in range(2, 2 * NS):
            f, h = H // 2, H % 2
            g.wait_ge(s_wfree, H - 1)
            g.dma_start(
                out=whbuf[:, (H % 2) * 16 * P:(H % 2 + 1) * 16 * P],
                in_=wr[f, :, 16 * h:16 * (h + 1), :].rearrange("p s o -> p (s o)"),
            ).then_inc(d_w, 16)
            if H % 2 == 1:
                f2 = H // 2
                g.wait_ge(s_evac, 4 * f2)
                g.dma_start(out=lAx2[:, 0:T], in_=xr[f2]).then_inc(d_xr, 16)

    _SERIAL_OPS = {"tensor_scalar", "tensor_tensor", "tensor_copy", "reciprocal",
                   "memset", "scalar_tensor_tensor", "tensor_reduce", "activation",
                   "copy", "iota", "tensor_single_scalar"}

    class _SerialEng:
        def __init__(self, eng):
            self._e = eng

        def __getattr__(self, name):
            attr = getattr(self._e, name)
            if callable(attr) and name in _SERIAL_OPS:
                def wrapped(*a, **k):
                    r = attr(*a, **k)
                    self._e.drain()
                    return r
                return wrapped
            return attr

    with nc.Block() as block:
        def _find(name):
            for i2, e in enumerate(plan):
                if e[4] == name:
                    return i2
            raise KeyError(name)

        def _extract(names):
            return [plan.pop(_find(nm)) for nm in names]

        stA = _extract([f"stu{u}" for u in range(NS)])
        stB = _extract([f"stu{u}" for u in range(NS, 2 * NS)])
        rowsA = _extract(["rowsA", "lnA", "mursA"])
        rowsB = _extract(["rowsB", "lnB", "mursB"])
        sqA = _extract(["sq0"])
        sqB = _extract(["sq1"])
        pos = _find("atil_mm") + 1
        for e in stA:
            plan.insert(pos, e)
            pos += 2
        for j, e in enumerate(rowsA):
            plan.insert(pos + j, e)
        pos += len(rowsA)
        plan.insert(pos, sqB[0]); pos += 1
        for e in stB:
            if pos < len(plan):
                plan.insert(pos, e); pos += 2
            else:
                plan.append(e); pos = len(plan)
        ise = _find("setup_end")
        for j, e in enumerate(rowsB):
            plan.insert(ise + j, e)
        plan.insert(_find("stp") + 1, sqA[0])
        finalize_plan()

        def emit(engine_name, eng):
            ci = 0
            for (e, kind, fn, serial, _nm) in plan:
                if kind == "chain":
                    if e == engine_name:
                        eng.wait_ge(ch, ci)
                        use_ser = serial and engine_name in ("vector", "scalar", "gpsimd")
                        fn(_SerialEng(eng) if use_ser else eng)
                        eng.drain().then_inc(ch, 1)
                    ci += 1
                else:
                    if e == engine_name:
                        fn(eng)

        @block.sync
        def _(sync):
            emit("sync", sync)

        @block.gpsimd
        def _(g):
            emit("gpsimd", g)

        @block.tensor
        def _(t):
            emit("tensor", t)

        @block.vector
        def _(v):
            emit("vector", v)

        @block.scalar
        def _(s):
            emit("scalar", s)

    return nc



def kernel(**inputs):
    x = np.ascontiguousarray(np.asarray(inputs["L0_attn_u"], np.float32))
    weight = np.asarray(inputs["L0_ffn_weight"], np.float32)
    lora_A = np.asarray(inputs["L0_ffn_lora_A"], np.float32)
    lora_B = np.asarray(inputs["L0_ffn_lora_B"], np.float32)
    gw1 = np.asarray(inputs["L0_ffn_gw1"], np.float32)

    xf = x.reshape(-1, DM)
    shared = {
        "wT": np.ascontiguousarray(weight.reshape(NS, P, NS, P).transpose(0, 3, 2, 1)),
        "lA": np.ascontiguousarray(lora_A.transpose(1, 0, 2).reshape(NS, P, P)),
        "lB": np.ascontiguousarray(lora_B.reshape(P, NS, P).transpose(1, 0, 2)),
        "g1": np.ascontiguousarray(gw1.T.reshape(NS, P, 32)),
        "g2T": np.ascontiguousarray(np.asarray(inputs["L0_ffn_gw2"], np.float32).T),
        "gb1": np.ascontiguousarray(np.asarray(inputs["L0_ffn_gb1"], np.float32).reshape(32, 1)),
        "lng": np.ascontiguousarray(np.asarray(inputs["L0_ffn_ln_gamma"], np.float32).reshape(32, 1)),
        "lnb": np.ascontiguousarray(np.asarray(inputs["L0_ffn_ln_beta"], np.float32).reshape(32, 1)),
        "gb2": np.ascontiguousarray(np.asarray(inputs["L0_ffn_gb2"], np.float32).reshape(8, 1)),
        "ssm": np.ascontiguousarray(np.concatenate([
            np.asarray(inputs["L0_attn_Lambda_ri"], np.float32),
            np.asarray(inputs["L0_attn_p_ri"], np.float32),
            np.asarray(inputs["L0_attn_q_ri"], np.float32),
            np.asarray(inputs["L0_attn_Ct_ri"], np.float32).reshape(64, 2),
            np.asarray(inputs["L0_attn_Bp"], np.float32).reshape(64, 1)], axis=1)),
        "vcr": np.ascontiguousarray(np.asarray(inputs["L0_attn_Vc_ri"], np.float32).transpose(1, 0, 2)[:, :, 0]),
        "vci": np.ascontiguousarray(np.asarray(inputs["L0_attn_Vc_ri"], np.float32).transpose(1, 0, 2)[:, :, 1]),
        "dsc": np.ascontiguousarray(np.concatenate([
            np.asarray(inputs["L0_attn_D"], np.float32).reshape(1),
            np.asarray(inputs["L0_attn_log_step"], np.float32).reshape(1)]).reshape(1, 2)),
    }

    nc = _build()
    in_maps = []
    for c in range(NCORES):
        m = {"xT": np.ascontiguousarray(xf[c * T:(c + 1) * T].T)}
        m.update(shared)
        in_maps.append(m)

    res = run_bass_kernel_spmd(nc, in_maps, core_ids=list(range(NCORES)))
    out = np.empty((NCORES * T, DM), np.float32)
    for c in range(NCORES):
        out[c * T:(c + 1) * T] = res.results[c]["outT"].T
    return out.reshape(8, T, DM)


